# version 16
# baseline (speedup 1.0000x reference)
import numpy as np
import ml_dtypes
from contextlib import ExitStack

import concourse.bass as bass
import concourse.mybir as mybir
from concourse.bass_utils import run_bass_kernel_spmd

F32 = mybir.dt.float32
BF16 = mybir.dt.bfloat16
I8 = mybir.dt.int8
AF = mybir.ActivationFunctionType
ALU = mybir.AluOpType
AX = mybir.AxisListType
BF = ml_dtypes.bfloat16

NCORES = 8
E = 2
S = 2048
D = 2048
H = 32
KVH = 8
HD = 64
J = 8192
NT = S // 128
NHC = D // 128
NJC = J // 128
QC = 512
NQC = S // QC
TB = 512
NBLK = E * S // TB
EPS = 1e-5

ENGS = ("tensor", "vector", "scalar", "gpsimd", "sync")

_EVEN = [h for h in range(H) if ((h // 4) % 2) == 0]
_ODD = [h for h in range(H) if ((h // 4) % 2) == 1]
CHUNK_OF = {}
OFF_OF = {}
for _c in range(NHC):
    CHUNK_OF[_EVEN[_c]] = _c
    OFF_OF[_EVEN[_c]] = 0
    CHUNK_OF[_ODD[_c]] = _c
    OFF_OF[_ODD[_c]] = 64


def _qhead_perm():
    perm = np.empty(D, dtype=np.int64)
    for c in range(NHC):
        for off, hh in ((0, _EVEN[c]), (64, _ODD[c])):
            for d in range(HD):
                perm[c * 128 + off + d] = hh * HD + d
    return perm


class Sem:
    def __init__(self, nc, es, name):
        self.h = es.enter_context(nc.semaphore(name))
        self.n = 0
        self.name = name


class Plan:
    def __init__(self):
        self.q = {e: [] for e in ENGS}
        self.last_wait = {e: {} for e in ENGS}

    def wait(self, eng, sem, val):
        if val is None or val <= 0:
            return
        lw = self.last_wait[eng]
        if lw.get(sem.name, 0) >= val:
            return
        lw[sem.name] = val
        self.q[eng].append(lambda EE, s=sem.h, v=val: EE.wait_ge(s, v))

    def do(self, eng, fn, wait=(), inc=None, inc_by=1):
        for (sem, val) in wait:
            self.wait(eng, sem, val)
        if inc is None:
            self.q[eng].append(fn)
            return None
        inc.n += inc_by
        self.q[eng].append(
            lambda EE, f=fn, s=inc.h, b=inc_by: f(EE).then_inc(s, b))
        return inc.n

    def dma(self, eng, out_ap, in_ap, sem, wait=()):
        return self.do(eng, lambda EE, o=out_ap, i=in_ap: EE.dma_start(o, i),
                       wait=wait, inc=sem, inc_by=16)

    def emit(self, nc):
        with nc.Block() as block:
            for eng in ENGS:
                def runner(EE, _eng=eng):
                    for f in self.q[_eng]:
                        f(EE)
                getattr(block, eng)(runner)


DEBUG_STAGE = None
DEBUG_QKV_LIMIT = None
DEBUG_NO_V = False
DEBUG_NO_ROPE = False


def build_nc():
    dbg = DEBUG_STAGE
    nc = bass.Bass("TRN2", target_bir_lowering=False)

    x_in = nc.dram_tensor("x", [E, S, D], F32, kind="ExternalInput")
    wqt = nc.dram_tensor("wqt", [D, D], F32, kind="ExternalInput")
    wkt_d = nc.dram_tensor("wkt", [D, KVH * HD], F32, kind="ExternalInput")
    wvt = nc.dram_tensor("wvt", [D, KVH * HD], F32, kind="ExternalInput")
    wot = nc.dram_tensor("wot", [D, D], F32, kind="ExternalInput")
    wg_d = nc.dram_tensor("wg", [D, J], I8, kind="ExternalInput")
    wu_d = nc.dram_tensor("wu", [D, J], I8, kind="ExternalInput")
    wd_d = nc.dram_tensor("wd", [J, D], I8, kind="ExternalInput")
    sg_in = nc.dram_tensor("sg", [128, NJC], F32, kind="ExternalInput")
    su_in = nc.dram_tensor("su", [128, NJC], F32, kind="ExternalInput")
    sdb_in = nc.dram_tensor("sdb", [128, D], F32, kind="ExternalInput")
    ident_in = nc.dram_tensor("ident", [128, 128], BF16, kind="ExternalInput")
    r2t_in = nc.dram_tensor("r2t", [128, 128], BF16, kind="ExternalInput")
    cost_in = nc.dram_tensor("cost", [128, S], F32, kind="ExternalInput")
    sint_in = nc.dram_tensor("sint", [128, S], F32, kind="ExternalInput")
    stair_in = nc.dram_tensor("stair", [128, 1024], BF16, kind="ExternalInput")
    onesr_in = nc.dram_tensor("onesr", [1, 128], BF16, kind="ExternalInput")
    onesc_in = nc.dram_tensor("onesc", [128, 128], BF16, kind="ExternalInput")

    out = nc.dram_tensor("out", [E, S, D], F32, kind="ExternalOutput")
    skind = dict(kind="ExternalOutput") if dbg else {}
    xnt_scr = nc.dram_tensor("xnt_scr", [E, NHC, 128, S], BF16, **skind)
    wg_bf = nc.dram_tensor("wg_bf", [16, 128, NHC * 512], BF16)
    wu_bf = nc.dram_tensor("wu_bf", [16, 128, NHC * 512], BF16)
    wd_bf = nc.dram_tensor("wd_bf", [8, 128, NJC * 256], BF16)
    proj_scr = nc.dram_tensor("proj_scr", [E, S, D], F32, **skind)

    p = Plan()
    es = ExitStack()

    dH = Sem(nc, es, "dH")
    dS = Sem(nc, es, "dS")
    pe = Sem(nc, es, "pe")
    vE = Sem(nc, es, "vE")
    sE = Sem(nc, es, "sE")
    dn = Sem(nc, es, "dn")
    gE = Sem(nc, es, "gE")
    dA = Sem(nc, es, "dA")
    all_sems = [dH, dS, pe, vE, sE, dn, gE, dA]
    nbarrier = [0]

    def barrier():
        for eng in ENGS:
            dn.n += 1
            p.q[eng].append(lambda EE, s=dn.h: EE.drain().then_inc(s, 1))
        tgt = dn.n
        for eng in ENGS:
            p.wait(eng, dn, tgt)
        nbarrier[0] += 1

    ident = es.enter_context(nc.sbuf_tensor("ident_sb", [128, 128], BF16))
    r2t = es.enter_context(nc.sbuf_tensor("r2t_sb", [128, 128], BF16))
    stair = es.enter_context(nc.sbuf_tensor("stair_sb", [128, 1024], BF16))
    onesr = es.enter_context(nc.sbuf_tensor("onesr_sb", [1, 128], BF16))
    onesc = es.enter_context(nc.sbuf_tensor("onesc_sb", [128, 128], BF16))
    cnegs = es.enter_context(nc.sbuf_tensor("cnegs_sb", [128, E * H], F32))
    sg_sb = es.enter_context(nc.sbuf_tensor("sg_sb", [128, NJC], F32))
    su_sb = es.enter_context(nc.sbuf_tensor("su_sb", [128, NJC], F32))

    p.dma("sync", ident[:, :], ident_in[:, :], dH)
    p.dma("sync", r2t[:, :], r2t_in[:, :], dH)
    p.dma("sync", stair[:, :], stair_in[:, :], dH)
    p.dma("sync", onesr[0:1, :], onesr_in[0:1, :], dH)
    p.dma("sync", onesc[:, :], onesc_in[:, :], dH)
    p.dma("sync", sg_sb[:, :], sg_in[:, :], dH)
    p.dma("sync", su_sb[:, :], su_in[:, :], dH)
    cosT = es.enter_context(nc.sbuf_tensor("cosT_sb", [128, S], F32))
    sinT = es.enter_context(nc.sbuf_tensor("sinT_sb", [128, S], F32))
    p.dma("sync", cosT[:, :], cost_in[:, :], dH)
    const_dma = p.dma("sync", sinT[:, :], sint_in[:, :], dH)
    for _eng in ENGS:
        p.wait(_eng, dH, const_dma)

    ot_scr = nc.dram_tensor("ot_scr", [E, NQC, 128, NHC * QC], BF16, **skind)
    if dbg:
        qdbg = nc.dram_tensor("qdbg", [E, 128, NHC * S], BF16, kind="ExternalOutput")
        ptdbg = nc.dram_tensor("ptdbg", [2, 128, 1024], BF16, kind="ExternalOutput")
        podbg = nc.dram_tensor("podbg", [65, QC], F32, kind="ExternalOutput")
        kdbg = nc.dram_tensor("kdbg", [E, 128, 4 * S], BF16, kind="ExternalOutput")
        vdbg = nc.dram_tensor("vdbg", [E, 128, NT * 520], BF16, kind="ExternalOutput")
        cdbg = nc.dram_tensor("cdbg", [128, E * H], F32, kind="ExternalOutput")

    def snap_waits():
        return [(pe, pe.n), (vE, vE.n), (sE, sE.n)]

    conv_state = [0]

    def p1_p2_elem(e):
        cneg = cnegs[:, e * H:(e + 1) * H]
        scope = ExitStack()
        sxnt = ExitStack()
        xnt = sxnt.enter_context(nc.sbuf_tensor(f"xnt{e}", [128, NHC * S], BF16,
                                                side="right"))

        sc1 = ExitStack()
        xb = [sc1.enter_context(nc.sbuf_tensor(f"xb{e}{i}", [128, D], F32,
                                               side="right")) for i in range(2)]
        xsq = [sc1.enter_context(nc.sbuf_tensor(f"xsq{e}{i}", [128, D], F32,
                                                side="right")) for i in range(2)]
        st = sc1.enter_context(nc.sbuf_tensor(f"st{e}", [128, 24], F32,
                                              side="right"))
        xnb = [sc1.enter_context(nc.sbuf_tensor(f"xnb{e}{i}", [128, D], BF16,
                                                side="right")) for i in range(2)]
        ptrp = [sc1.enter_context(nc.psum_tensor(f"ptr{e}{i}", [128, D], F32))
                for i in range(2)]
        x_done = [0] * NT
        xn_done = [0] * NT
        tr_done = [0] * NT
        ev_done = [0] * NT
        sq_done = [0] * NT
        r2_done = [0] * NT
        ch_done = [0] * NT

        def ln_stats(tt):
            pr = tt % 2
            so = pr * 12
            c_r1 = p.do("vector", lambda EE, b=xb[pr], s=so: EE.reduce_sum(
                st[:, s:s + 1], b[:, :], axis=AX.X),
                wait=[(dH, x_done[tt])], inc=vE)
            w = [(dH, x_done[tt])]
            if tt >= 2:
                w.append((vE, r2_done[tt - 2]))
            c_sq = p.do("scalar", lambda EE, b=xb[pr], o=xsq[pr]: EE.activation(
                o[:, :], b[:, :], AF.Square), wait=w, inc=sE)
            sq_done[tt] = c_sq
            r2_done[tt] = p.do("vector", lambda EE, i_=xsq[pr], s=so:
                               EE.reduce_sum(
                st[:, s + 1:s + 2], i_[:, :], axis=AX.X),
                wait=[(sE, c_sq), (vE, c_r1)], inc=vE)

        def ln_chain(tt):
            pr = tt % 2
            so = pr * 12
            c = r2_done[tt]
            c = p.do("vector", lambda EE, s=so: EE.tensor_scalar(
                st[:, s + 2:s + 3], st[:, s:s + 1], 1.0 / D, None, ALU.mult),
                wait=[(vE, c)], inc=vE)
            c = p.do("vector", lambda EE, s=so: EE.tensor_scalar(
                st[:, s + 3:s + 4], st[:, s + 1:s + 2], 1.0 / D, EPS,
                ALU.mult, ALU.add), wait=[(vE, c)], inc=vE)
            c = p.do("vector", lambda EE, s=so: EE.tensor_tensor(
                st[:, s + 4:s + 5], st[:, s + 2:s + 3], st[:, s + 2:s + 3],
                ALU.mult), wait=[(vE, c)], inc=vE)
            c_var = p.do("vector", lambda EE, s=so: EE.tensor_tensor(
                st[:, s + 5:s + 6], st[:, s + 3:s + 4], st[:, s + 4:s + 5],
                ALU.subtract), wait=[(vE, c)], inc=vE)
            c_ln = p.do("scalar", lambda EE, s=so: EE.activation(
                st[:, s + 6:s + 7], st[:, s + 5:s + 6], AF.Ln),
                wait=[(vE, c_var)], inc=sE)
            c_rs = p.do("scalar", lambda EE, s=so: EE.activation(
                st[:, s + 9:s + 10], st[:, s + 6:s + 7], AF.Exp, scale=-0.5),
                wait=[(sE, c_ln)], inc=sE)
            ch_done[tt] = p.do("vector", lambda EE, s=so: EE.tensor_scalar(
                st[:, s + 10:s + 11], st[:, s + 2:s + 3], st[:, s + 9:s + 10],
                -1.0, ALU.mult, ALU.mult), wait=[(sE, c_rs)], inc=vE)

        def ln_xn(tt):
            pr = tt % 2
            so = pr * 12
            w = [(vE, ch_done[tt])]
            if tt >= 2:
                w.append((pe, tr_done[tt - 2]))
            xn_done[tt] = p.do("scalar", lambda EE, b=xb[pr], o=xnb[pr], s=so:
                               EE.activation(
                o[:, :], b[:, :], AF.Identity,
                scale=st[:, s + 9:s + 10], bias=st[:, s + 10:s + 11]),
                wait=w, inc=sE)

        def ln_tr(tt):
            pr = tt % 2
            w = [(sE, xn_done[tt])]
            if tt >= 2:
                w.append((vE, ev_done[tt - 2]))
            for hc in range(NHC):
                tr_done[tt] = p.do("tensor", lambda EE, h=hc, i_=xnb[pr],
                                   o=ptrp[pr]:
                                   EE.matmul(o[:, h * 128:(h + 1) * 128],
                                             i_[:, h * 128:(h + 1) * 128],
                                             ident[:, :], start=True, stop=True),
                                   wait=w, inc=pe if hc == NHC - 1 else None
                                   ) or tr_done[tt]
                w = []

        def ln_ev(tt):
            pr = tt % 2
            ev_done[tt] = p.do("vector", lambda EE, t=tt, o=ptrp[pr]:
                               EE.tensor_copy(
                xnt[:, :].rearrange("p (c t) -> p c t", t=S)[
                    :, :, t * 128:(t + 1) * 128],
                o[:, :].rearrange("p (c u) -> p c u", u=128)),
                wait=[(pe, tr_done[tt])], inc=vE)

        for tt in range(NT + 1):
            if tt < NT:
                pr = tt % 2
                w = [(dH, const_dma)]
                if tt >= 2:
                    w += [(sE, xn_done[tt - 2]), (vE, ev_done[tt - 2]),
                          (vE, r2_done[tt - 2])]
                if e > 0:
                    w += snap_prev
                x_done[tt] = p.dma("sync", xb[pr][:, :],
                                   x_in[e, tt * 128:(tt + 1) * 128, :], dH,
                                   wait=w)
                ln_stats(tt)
                ln_chain(tt)
                ln_xn(tt)
                ln_tr(tt)
            if tt >= 1:
                ln_ev(tt - 1)
        xnt_wr = 0
        for c in range(NHC):
            xnt_wr = p.dma("sync", xnt_scr[e, c], xnt[:, c * S:(c + 1) * S], dH,
                           wait=[(vE, ev_done[NT - 1])])
        sc1.close()
        if dbg == 1:
            sxnt.close()
            scope.close()
            return snap_waits() + [(dH, dH.n), (dS, dS.n)]
        qt = scope.enter_context(nc.sbuf_tensor(f"qt{e}", [128, NHC * S], BF16))
        kt4 = scope.enter_context(nc.sbuf_tensor(f"kt4{e}", [128, 4 * S], BF16))

        sc2 = ExitStack()
        wbuf = [sc2.enter_context(nc.sbuf_tensor(f"wqb{e}{i}", [128, NHC * 128],
                                                 BF16, side="right"))
                for i in range(2)]
        qbf = [sc2.enter_context(nc.sbuf_tensor(f"qbf{e}{i}", [128, S], BF16,
                                                side="right")) for i in range(2)]
        t1sb = sc2.enter_context(nc.sbuf_tensor(f"t1{e}", [128, 1024], F32,
                                                side="right"))
        t2sb = sc2.enter_context(nc.sbuf_tensor(f"t2{e}", [128, 1024], F32,
                                                side="right"))
        psA2 = [sc2.enter_context(nc.psum_tensor(f"pqA{e}{i}", [128, 1024], F32))
                for i in range(2)]
        psB2 = [sc2.enter_context(nc.psum_tensor(f"pqB{e}{i}", [128, 1024], F32))
                for i in range(2)]

        wsrcs = [(wqt, oc, qt, oc) for oc in range(NHC)] + \
                [(wkt_d, oc, kt4, oc) for oc in range(4)]
        if DEBUG_QKV_LIMIT is not None:
            wsrcs = wsrcs[:DEBUG_QKV_LIMIT]
        wdma = [0, 0]
        r2_read = [0, 0]
        qkv_done = [0, 0]
        qbf_done = [0, 0]
        t1_done = [0, 0]
        c_t2_done = [0, 0]
        rope_done = 0
        nhalf = 2 * len(wsrcs)

        def p1b_qkv(j):
            oc = j // 2
            par = oc % 2
            hp = j % 2
            g = (j % 2) * 1024
            wsrc, ocx, dstt, doc = wsrcs[oc]
            if j % 2 == 0:
                w = [(vE, ev_done[NT - 1])]
                if oc >= 2:
                    w.append((pe, qkv_done[par]))
                for hh in range(2):
                    wdma[par] = p.dma(
                        "gpsimd",
                        wbuf[par][:, hh * 8 * 128:(hh + 1) * 8 * 128].rearrange(
                            "p (c o) -> p c o", o=128),
                        wsrc.rearrange("(c p) o -> p c o", p=128)[
                            :, hh * 8:(hh + 1) * 8, ocx * 128:(ocx + 1) * 128],
                        dS, wait=w)
                    w = []
            w = [(dS, wdma[par]), (dH, xnt_wr),
                 (vE, t1_done[hp]), (sE, qbf_done[hp])]
            mm = 0
            for hc in range(NHC):
                for tc in range(2):
                    last = (hc == NHC - 1 and tc == 1)
                    mm = p.do("tensor", lambda EE, h=hc, t=tc, wb=wbuf[par],
                              g_=g, hpp=hp: EE.matmul(
                        psA2[hpp][:, t * 512:(t + 1) * 512],
                        wb[:, h * 128:(h + 1) * 128],
                        xnt[:, h * S + g_ + t * 512: h * S + g_ + (t + 1) * 512],
                        start=(h == 0), stop=(h == NHC - 1)), wait=w,
                        inc=pe if last else None) or mm
                    w = []
            qkv_done[par] = mm
            return mm

        def p1b_tail(j, mm_j):
            nonlocal rope_done
            oc = j // 2
            hp = j % 2
            g = (j % 2) * 1024
            wsrc, ocx, dstt, doc = wsrcs[oc]
            qbf_done[hp] = p.do("scalar", lambda EE, o=qbf[oc % 2], g_=g, hpp=hp:
                                EE.copy(o[:, g_:g_ + 1024], psA2[hpp][:, :]),
                                wait=[(pe, mm_j), (pe, r2_read[hp])], inc=sE)
            r2mm = 0
            w = [(sE, qbf_done[hp]), (vE, c_t2_done[hp])]
            for tc in range(2):
                r2mm = p.do("tensor", lambda EE, t=tc, qb=qbf[oc % 2], g_=g,
                            hpp=hp: EE.matmul(
                    psB2[hpp][:, t * 512:(t + 1) * 512], r2t[:, :],
                    qb[:, g_ + t * 512: g_ + (t + 1) * 512],
                    start=True, stop=True),
                    wait=w, inc=pe if tc == 1 else None) or r2mm
                w = []
            r2_read[hp] = r2mm
            t1_done[hp] = p.do("vector", lambda EE, g_=g, hpp=hp: EE.tensor_tensor(
                t1sb[:, :], psA2[hpp][:, :], cosT[:, g_:g_ + 1024], ALU.mult),
                wait=[(pe, mm_j), (sE, qbf_done[hp])], inc=vE)
            c_t2 = 0
            for tcq in range(2):
                c_t2 = p.do("vector", lambda EE, t=tcq, g_=g, hpp=hp:
                            EE.tensor_tensor(
                    t2sb[:, t * 512:(t + 1) * 512],
                    psB2[hpp][:, t * 512:(t + 1) * 512],
                    sinT[:, g_ + t * 512: g_ + (t + 1) * 512], ALU.mult),
                    wait=[(pe, r2mm)], inc=vE)
            c_t2_done[hp] = c_t2
            for tcq in range(2):
                rope_done = p.do(
                    "vector", lambda EE, dt=dstt, o=doc, t=tcq, g_=g:
                    EE.tensor_tensor(
                        dt[:, o * S + g_ + t * 512: o * S + g_ + (t + 1) * 512],
                        t1sb[:, t * 512:(t + 1) * 512],
                        t2sb[:, t * 512:(t + 1) * 512], ALU.add),
                    wait=[(vE, c_t2)], inc=vE)

        mm_hist = {}
        for j in range(nhalf + 1):
            if j < nhalf:
                mm_hist[j] = p1b_qkv(j)
            if j >= 1:
                p1b_tail(j - 1, mm_hist[j - 1])
        sc2.close()

        v65 = scope.enter_context(nc.sbuf_tensor(f"v65_{e}", [128, NT * 520 + 64],
                                                 BF16))
        v_done = rope_done
        sc2b = ExitStack()
        wv_sb = sc2b.enter_context(nc.sbuf_tensor(f"wv{e}", [128, NHC * 512],
                                                  BF16, side="right"))
        pvA = sc2b.enter_context(nc.psum_tensor(f"pvA{e}", [128, 512], F32))
        pvB = sc2b.enter_context(nc.psum_tensor(f"pvB{e}", [128, 512], F32))
        wv_dma = 0
        w = [(pe, qkv_done[0]), (pe, qkv_done[1]),
             (pe, r2_read[0]), (pe, r2_read[1]),
             (sE, qbf_done[0]), (sE, qbf_done[1]),
             (vE, rope_done), (vE, t1_done[0]), (vE, t1_done[1]),
             (vE, c_t2_done[0]), (vE, c_t2_done[1])]
        for hh in range(2 if not DEBUG_NO_V else 0):
            wv_dma = p.dma(
                "gpsimd",
                wv_sb[:, hh * 8 * 512:(hh + 1) * 8 * 512].rearrange(
                    "p (c o) -> p c o", o=512),
                wvt.rearrange("(c p) o -> p c o", p=128)[:, hh * 8:(hh + 1) * 8, :],
                dS, wait=w)
            w = []
        if e == 0:
            for (src_d, dst_d) in ((wg_d, wg_bf), (wu_d, wu_bf)):
                for s_ in range(16):
                    conv_state[0] = p.dma(
                        "gpsimd",
                        dst_d[s_].rearrange("p (c j) -> p c j", j=512),
                        src_d.rearrange("(c p) (s j) -> s p c j", p=128, j=512)[s_],
                        dS)
            for m_ in range(8):
                conv_state[0] = p.dma(
                    "gpsimd",
                    wd_bf[m_].rearrange("p (c m) -> p c m", m=256),
                    wd_d.rearrange("(c p) (m m2) -> m p c m2", p=128, m2=256)[m_],
                    dS)
        v_ev = [0] * NT
        for tt in range(NT if not DEBUG_NO_V else 0):
            pr = tt % 2
            psv = pvA if pr == 0 else pvB
            w = [(dS, wv_dma), (vE, rope_done),
                 (sE, qbf_done[0]), (sE, qbf_done[1])]
            if tt >= 2:
                w.append((vE, v_ev[tt - 2]))
            mm = 0
            for hc in range(NHC):
                mm = p.do("tensor", lambda EE, h=hc, t=tt, o=psv: EE.matmul(
                    o[:, 0:512],
                    xnt[:, h * S + t * 128: h * S + t * 128 + 128],
                    wv_sb[:, h * 512:(h + 1) * 512],
                    start=(h == 0), stop=(h == NHC - 1)), wait=w,
                    inc=pe if hc == NHC - 1 else None) or mm
                w = []
            c_ms = p.do("vector", lambda EE, t=tt: EE.memset(
                v65[:, t * 520:(t + 1) * 520], 1.0), wait=[(pe, mm)], inc=vE)
            p.wait("vector", vE, c_ms)
            v_ev[tt] = p.do("vector", lambda EE, t=tt, o=psv: EE.tensor_copy(
                v65[:, t * 520:(t + 1) * 520].rearrange(
                    "p (h j) -> p h j", j=65)[:, :, 0:64],
                o[:, 0:512].rearrange("p (h j) -> p h j", j=64)), inc=vE)
        v_done = v_ev[NT - 1] if not DEBUG_NO_V else rope_done
        sc2b.close()
        sxnt.close()
        if dbg == 2:
            p.dma("sync", qdbg[e], qt[:, :], dH, wait=[(vE, rope_done)])
            p.dma("sync", kdbg[e], kt4[:, :], dH)
            p.dma("sync", vdbg[e], v65[:, :], dH, wait=[(vE, v_done)])
            scope.close()
            return snap_waits() + [(dH, dH.n), (dS, dS.n)]

        sc3 = ExitStack()
        qsq = [sc3.enter_context(nc.sbuf_tensor(f"qsq{e}{i}", [128, S], BF16,
                                                side="right")) for i in range(2)]
        n2 = sc3.enter_context(nc.sbuf_tensor(f"n2{e}", [1, H + KVH + 1], F32,
                                              side="right"))
        negc = sc3.enter_context(nc.sbuf_tensor(f"negc{e}", [1, H], BF16,
                                                side="right"))
        with nc.psum_tensor(f"pb{e}0", [128, S], F32) as pbA, \
             nc.psum_tensor(f"pb{e}1", [128, S], F32) as pbB:
            pb = [pbA, pbB]
            pb_free = [0, 0]
            sq_used = 0
            last_red = 0
            chunk_list = [(qt, c, (0, _EVEN[c]), (64, _ODD[c]))
                          for c in range(NHC)] + \
                         [(kt4, c, (0, H + 2 * c), (64, H + 2 * c + 1))
                          for c in range(4)]
            chunks = chunk_list
            sq_done_c = {}
            sq_used_p = [0, 0]
            for ci, (srct, c, ha, hb) in enumerate(chunks):
                par = ci % 2
                sq_done_c[ci] = p.do("vector", lambda EE, s=srct, cc=c, pq=par:
                                     EE.tensor_tensor(
                    qsq[pq][:, :], s[:, cc * S:(cc + 1) * S],
                    s[:, cc * S:(cc + 1) * S], ALU.mult),
                    wait=[(vE, rope_done), (vE, v_done), (pe, sq_used_p[par])],
                    inc=vE)
                for hi, (off, col) in enumerate((ha, hb)):
                    hpar = (2 * ci + hi) % 2
                    mmb = 0
                    for tc in range(4):
                        mmb = p.do("tensor", lambda EE, o=off, t=tc,
                                   b=pb[hpar], pq=par: EE.matmul(
                                       b[0:128, t * 512:(t + 1) * 512],
                                       onesc[o:o + 64, 0:128],
                                       qsq[pq][o:o + 64, t * 512:(t + 1) * 512],
                                       start=True, stop=True),
                                   wait=([(vE, sq_done_c[ci]),
                                          (vE, pb_free[hpar])]
                                         if tc == 0 else []),
                                   inc=pe if tc == 3 else None) or mmb
                    sq_used_p[par] = mmb
                    pb_free[hpar] = last_red = p.do(
                        "vector", lambda EE, cl=col, b=pb[hpar]: EE.reduce_max(
                            n2[0:1, cl:cl + 1], b[0:1, :], axis=AX.X),
                        wait=[(pe, mmb)], inc=vE)
            p.wait("vector", vE, last_red)
        c_c2 = 0
        for kv in range(KVH):
            c_c2 = p.do("vector", lambda EE, k=kv: EE.tensor_scalar(
                n2[0:1, 4 * k:4 * k + 4], n2[0:1, 4 * k:4 * k + 4],
                n2[0:1, H + k:H + k + 1], None, ALU.mult),
                wait=[(vE, last_red)], inc=vE)
        p.wait("vector", vE, c_c2)
        c_sqt = p.do("scalar", lambda EE: EE.activation(
            n2[0:1, 0:H], n2[0:1, 0:H], AF.Sqrt), wait=[(vE, c_c2)], inc=sE)
        c_ng = p.do("vector", lambda EE: EE.tensor_scalar(
            negc[0:1, :], n2[0:1, 0:H], -0.125, None, ALU.mult),
            wait=[(sE, c_sqt)], inc=vE)
        with nc.psum_tensor(f"pcn{e}", [128, H], F32) as pcn:
            c_bc = p.do("tensor", lambda EE: EE.matmul(
                pcn[:, :], onesr[0:1, :], negc[0:1, :], start=True, stop=True),
                wait=[(vE, c_ng)], inc=pe)
            c_cneg = p.do("vector", lambda EE: EE.tensor_copy(
                cneg[:, :], pcn[:, :]), wait=[(pe, c_bc)], inc=vE)
        sc3.close()
        if dbg is not None and dbg >= 3:
            p.dma("sync", cdbg[:, :], cnegs[:, :], dH, wait=[(vE, c_cneg)])
            p.dma("sync", qdbg[e], qt[:, :], dH)
            p.dma("sync", kdbg[e], kt4[:, :], dH)
            p.dma("sync", vdbg[e], v65[:, :], dH, wait=[(vE, v_done)])
        if dbg == 3:
            scope.close()
            return snap_waits() + [(dH, dH.n), (dS, dS.n)]

        sc4 = ExitStack()
        ktz = sc4.enter_context(nc.sbuf_tensor(f"ktz{e}", [128, KVH * S], BF16))
        kz_ms = p.do("vector", lambda EE: EE.memset(ktz[:, :], 0.0),
                     wait=[(vE, v_done), (vE, c_cneg)], inc=vE)
        p.wait("vector", vE, kz_ms)
        kz_done = 0
        for kvh_ in range(KVH):
            r0 = (kvh_ % 2) * 64
            kz_done = p.do("vector", lambda EE, kv=kvh_, r=r0: EE.tensor_copy(
                ktz[r:r + 64, kv * S:(kv + 1) * S],
                kt4[r:r + 64, (kv // 2) * S:(kv // 2 + 1) * S]), inc=vE)
        otq = sc4.enter_context(nc.sbuf_tensor(f"otq{e}", [128, NHC * QC], BF16,
                                               side="right"))
        ptb = [sc4.enter_context(nc.sbuf_tensor(f"pt{e}{i}", [128, 1024], BF16,
                                                side="right")) for i in range(4)]
        rsinv = sc4.enter_context(nc.sbuf_tensor(f"rsv{e}", [1, QC], BF16,
                                                 side="right"))
        invb = sc4.enter_context(nc.sbuf_tensor(f"invb{e}", [64, QC], F32,
                                                side="right"))
        pS = [sc4.enter_context(nc.psum_tensor(f"pS{e}{i}", [128, 1024], F32))
              for i in range(2)]
        pO = [sc4.enter_context(nc.psum_tensor(f"pO{e}{i}", [128, QC], F32))
              for i in range(2)]
        pI = sc4.enter_context(nc.psum_tensor(f"pI{e}", [128, QC], F32))

        exp_done = [0, 0]
        s_done = [0, 0]
        o_mm = [0, 0]
        o_mm4 = [0, 0, 0, 0]
        mask_done = [0, 0, 0, 0]
        ot_done = [0, 0]
        inv_mm = 0
        ib_done = 0
        ri_done = 0
        otq_dma = [0]

        steps = []
        for qc in range(NQC):
            nkt = 4 * qc + 4
            for h in range(H):
                for kp in range(nkt // 2):
                    steps.append((qc, h, kp, nkt))

        def emit_S(i):
            qc, h, kp, nkt = steps[i]
            kvh = h // 4
            qch, qoff = CHUNK_OF[h], OFF_OF[h]
            spar = i % 2
            w = [(vE, kz_done), (sE, exp_done[spar]), (vE, v_done)]
            s_mm = 0
            for half in range(2):
                ki = kp * 2 + half
                s_mm = p.do("tensor", lambda EE, k=ki, hf=half, sp=spar,
                            kv=kvh, qh_=qch, qq=qc: EE.matmul(
                    pS[sp][:, hf * 512:(hf + 1) * 512],
                    ktz[:, kv * S + k * 128: kv * S + k * 128 + 128],
                    qt[:, qh_ * S + qq * QC: qh_ * S + (qq + 1) * QC],
                    start=True, stop=True), wait=w,
                    inc=pe if half == 1 else None) or s_mm
                w = []
            s_done[spar] = s_mm

        def emit_tail(i):
            qc, h, kp, nkt = steps[i]
            kvh = h // 4
            hpar = h % 2
            spar = i % 2
            ptp = i % 4
            exp_done[spar] = p.do("scalar", lambda EE, sp=spar, pt=ptp, hh=h:
                                  EE.activation(
                                      ptb[pt][:, :], pS[sp][:, :], AF.Exp,
                                      scale=0.125, bias=cneg[:, hh:hh + 1]),
                                  wait=[(pe, s_done[spar]), (pe, o_mm4[ptp])],
                                  inc=sE)
            vv = 0
            for half in range(2):
                di = kp * 2 + half - 4 * qc
                if di >= 0:
                    nc_ = 128 * (di + 1)
                    vv = p.do("vector", lambda EE, pt=ptp, hf=half, d=di,
                              n_=nc_: EE.tensor_tensor(
                                  ptb[pt][:, hf * 512: hf * 512 + n_],
                                  ptb[pt][:, hf * 512: hf * 512 + n_],
                                  stair[:, 512 - 128 * d: 512 - 128 * d + n_],
                                  ALU.mult),
                              wait=[(sE, exp_done[spar])], inc=vE)
            if vv:
                mask_done[ptp] = vv
            w = [(sE, exp_done[spar])]
            if vv:
                w.append((vE, vv))
            if kp == 0:
                w.append((vE, ot_done[hpar]))
            for half in range(2):
                ki = kp * 2 + half
                o_mm4[ptp] = p.do("tensor", lambda EE, k=ki, hp=hpar, pt=ptp,
                                  hf=half, kv=kvh, nk=nkt: EE.matmul(
                                      pO[hp][:, :],
                                      v65[:, k * 520 + kv * 65: k * 520 + kv * 65 + 128],
                                      ptb[pt][:, hf * 512:(hf + 1) * 512],
                                      start=(k == 0), stop=(k == nk - 1)),
                                  wait=w, inc=pe if half == 1 else None
                                  ) or o_mm4[ptp]
                w = []
            o_mm[spar] = o_mm4[ptp]

        def emit_recip(i):
            nonlocal ri_done
            qc, h, kp, nkt = steps[i]
            hpar = h % 2
            def _recip(EE, hp=hpar):
                with nc.allow_low_precision("bf16 inv-rowsum operand for "
                                            "K=1 broadcast matmul"):
                    return EE.reciprocal(rsinv[0:1, :], pO[hp][64:65, :])
            ri_done = p.do("vector", _recip,
                           wait=[(pe, o_mm[0]), (pe, o_mm[1]),
                                 (pe, inv_mm)], inc=vE)

        def emit_norm(i):
            nonlocal inv_mm, ib_done
            qc, h, kp, nkt = steps[i]
            qch, qoff = CHUNK_OF[h], OFF_OF[h]
            hpar = h % 2
            inv_mm = p.do("tensor", lambda EE: EE.matmul(
                pI[:, :], onesr[0:1, 0:128], rsinv[0:1, :], start=True, stop=True),
                wait=[(vE, ri_done), (vE, ib_done)], inc=pe)
            for half in range(2):
                ib_done = p.do("vector", lambda EE, hf=half: EE.tensor_copy(
                    invb[:, hf * 256:(hf + 1) * 256],
                    pI[0:64, hf * 256:(hf + 1) * 256]),
                    wait=[(pe, inv_mm)], inc=vE)
            ww = [(dH, otq_dma[0]), (vE, ib_done)]
            for half in range(2):
                ot_done[hpar] = p.do(
                    "vector", lambda EE, hp=hpar, qh=qch, qo=qoff, hf=half:
                    EE.tensor_tensor(
                        otq[qo:qo + 64, qh * QC + hf * 256: qh * QC + (hf + 1) * 256],
                        pO[hp][0:64, hf * 256:(hf + 1) * 256],
                        invb[:, hf * 256:(hf + 1) * 256], ALU.mult),
                    wait=ww, inc=vE)
                ww = []

        nsteps = len(steps)
        for i in range(nsteps + 2):
            if i < nsteps:
                emit_S(i)
            if i >= 1 and i - 1 < nsteps:
                emit_tail(i - 1)
                qc, h, kp, nkt = steps[i - 1]
                if kp == nkt // 2 - 1:
                    emit_recip(i - 1)
            if i >= 2 and i - 2 < nsteps:
                j = i - 2
                qc, h, kp, nkt = steps[j]
                if kp == nkt // 2 - 1:
                    emit_norm(j)
                    if h == H - 1:
                        ot_all = max(ot_done)
                        otq_dma[0] = p.dma("sync", ot_scr[e, qc], otq[:, :],
                                           dH, wait=[(vE, ot_all)])
        sc4.close()
        scope.close()
        if dbg == 4:
            return snap_waits() + [(dH, dH.n), (dS, dS.n)]

        sc5 = ExitStack()
        wot_sb = sc5.enter_context(nc.sbuf_tensor(f"wot{e}", [128, NHC * D], BF16,
                                                  side="right"))
        otq2 = sc5.enter_context(nc.sbuf_tensor(f"otq2{e}", [128, NHC * QC], BF16,
                                                side="right"))
        prst = [sc5.enter_context(nc.sbuf_tensor(f"prs{e}{i}", [128, D], F32,
                                                 side="right")) for i in range(2)]
        pP = [sc5.enter_context(nc.psum_tensor(f"pP{e}{i}", [128, 512], F32))
              for i in range(2)]
        w = snap_waits() + [(dH, otq_dma[0])]
        wot_dma = 0
        for cp in range(4):
            wot_dma = p.dma(
                "gpsimd",
                wot_sb[:, cp * 4 * D:(cp + 1) * 4 * D].rearrange(
                    "p (c m) -> p c m", m=D),
                wot.rearrange("(c p) m -> p c m", p=128)[:, cp * 4:(cp + 1) * 4, :],
                dS, wait=w)
            w = []
        proj_last = [0, 0]
        prst_ev = [0, 0]
        prst_dma = [0, 0]
        otq2_dma = 0
        for qc in range(NQC):
            w2 = snap_waits() if qc == 0 else [(pe, max(proj_last))]
            otq2_dma = p.dma("sync", otq2[:, :], ot_scr[e, qc], dH, wait=w2)
            for tt in range(4):
                ppar = tt % 2
                for mc in range(4):
                    w = [(dH, otq2_dma), (dS, wot_dma), (vE, prst_ev[ppar])]
                    for oc in range(NHC):
                        proj_last[ppar] = p.do(
                            "tensor", lambda EE, o=oc, t=tt, m=mc, pp=ppar: EE.matmul(
                                pP[pp][:, :],
                                otq2[:, o * QC + t * 128: o * QC + t * 128 + 128],
                                wot_sb[:, o * D + m * 512: o * D + (m + 1) * 512],
                                start=(o == 0), stop=(o == NHC - 1)),
                            wait=w, inc=pe if oc == NHC - 1 else None
                            ) or proj_last[ppar]
                        w = []
                    prst_ev[ppar] = p.do("vector", lambda EE, m=mc, pp=ppar:
                                         EE.tensor_copy(
                                             prst[pp][:, m * 512:(m + 1) * 512],
                                             pP[pp][:, :]),
                                         wait=[(pe, proj_last[ppar]),
                                               (dH, prst_dma[ppar])], inc=vE)
                prst_dma[ppar] = p.dma(
                    "sync",
                    proj_scr[e, qc * QC + tt * 128: qc * QC + tt * 128 + 128, :],
                    prst[ppar][:, :], dH, wait=[(vE, prst_ev[ppar])])
        sc5.close()
        if dbg == 5:
            return snap_waits() + [(dH, dH.n), (dS, dS.n)]
        return snap_waits() + [(dH, dH.n), (dS, dS.n)]

    snap_prev = []
    for e in range(E):
        snap_prev = p1_p2_elem(e)
    barrier()
    if dbg is not None and dbg < 6:
        p.wait("sync", dH, dH.n)
        p.wait("gpsimd", dS, dS.n)
        p.wait("scalar", dA, dA.n)
        for s_ in all_sems:
            p.do("sync", lambda EE, hh=s_.h: EE.sem_clear(hh))
        p.emit(nc)
        es.close()
        return nc

    sc5 = ExitStack()
    fuse = sc5.enter_context(nc.sbuf_tensor("fuse", [128, NJC * 512], BF16))
    sdb_sb = sc5.enter_context(nc.sbuf_tensor("sdb_sb", [128, D], F32))
    p.dma("sync", sdb_sb[:, :], sdb_in[:, :], dH)
    wbig = sc5.enter_context(nc.sbuf_tensor("wbig", [128, NJC * 512], BF16))
    xn_tb = [sc5.enter_context(nc.sbuf_tensor("xntb0", [128, NHC * 512], BF16))]
    xn_tb.append(xn_tb[0])
    usb = [sc5.enter_context(nc.sbuf_tensor(f"usb{i}", [128, 512], F32))
           for i in range(2)]
    silb = [sc5.enter_context(nc.sbuf_tensor(f"silb{i}", [128, 512], F32))
            for i in range(2)]
    xtile = [sc5.enter_context(nc.sbuf_tensor(f"xt3{i}", [128, 256], F32))
             for i in range(4)]
    dtile = [sc5.enter_context(nc.sbuf_tensor(f"dt3{i}", [128, 256], F32))
             for i in range(4)]
    prtile = [sc5.enter_context(nc.sbuf_tensor(f"prt3{i}", [128, 256], F32))
              for i in range(4)]
    otile = [sc5.enter_context(nc.sbuf_tensor(f"ot3{i}", [128, 256], F32))
             for i in range(4)]
    pg = [sc5.enter_context(nc.psum_tensor(f"pg{i}", [128, 512], F32))
          for i in range(2)]
    pu = [sc5.enter_context(nc.psum_tensor(f"pu{i}", [128, 512], F32))
          for i in range(2)]
    pd = [sc5.enter_context(nc.psum_tensor(f"pd{i}", [128, 512], F32))
          for i in range(2)]

    def wg_view(par):
        return wbig[:, par * 16 * 512:(par + 1) * 16 * 512]

    def wu_view(par):
        return wbig[:, (2 + par) * 16 * 512:(3 + par) * 16 * 512]

    xn_dma = [0, 0]
    span_dma = [0, 0]
    span_free = [0, 0]
    sil_done = [0, 0]
    u_done = [0, 0]
    fuse_written = [0] * NJC
    gu_last_mm = 0
    dn_last_mm = 0
    ev_last = [0, 0, 0, 0]
    ev_last_cb = [0, 0, 0, 0]
    pd_free = [0, 0]
    gadd_done = [0, 0, 0, 0]
    dn_half_mm = [0, 0]
    wd_dma = [0, 0]
    x_dma = [0, 0, 0, 0]
    pr_dma = [0, 0, 0, 0]
    o_dma = [0, 0, 0, 0]
    out_total = 0

    for b in range(NBLK):
        e = b // (NBLK // E)
        tb = b % (NBLK // E)
        bpar = b % 2
        w = [(pe, gu_last_mm)] if b >= 1 else []
        for hh in range(2):
            xn_dma[bpar] = p.dma(
                "sync",
                xn_tb[bpar][:, hh * 8 * 512:(hh + 1) * 8 * 512].rearrange(
                    "p (c t) -> p c t", t=512),
                xnt_scr[e].rearrange("c p t -> p c t")[
                    :, hh * 8:(hh + 1) * 8, tb * 512:(tb + 1) * 512],
                dH, wait=w)
            w = []
        for s_ in range(16):
            spar = s_ % 2
            w = [(pe, span_free[spar])]
            if b > 0 and s_ < 2:
                w.append((pe, dn_last_mm))
            w.append((dS, conv_state[0]))
            for (wdram, view) in ((wg_bf, wg_view(spar)), (wu_bf, wu_view(spar))):
                span_dma[spar] = p.dma("sync", view[:, :], wdram[s_], dH, wait=w)
                w = []
            for j4 in range(4):
                jc = s_ * 4 + j4
                gpar = jc % 2
                wmm = [(dH, span_dma[spar]), (dH, xn_dma[bpar]),
                       (sE, sil_done[gpar])]
                gm = 0
                for hc in range(NHC):
                    gm = p.do("tensor", lambda EE, h=hc, j=j4, sp=spar, gp=gpar,
                              bp=bpar:
                              EE.matmul(
                                  pg[gp][:, :],
                                  wg_view(sp)[:, h * 512 + j * 128: h * 512 + j * 128 + 128],
                                  xn_tb[bp][:, h * 512:(h + 1) * 512],
                                  start=(h == 0), stop=(h == NHC - 1)),
                              wait=wmm, inc=pe if hc == NHC - 1 else None) or gm
                    wmm = []
                wmm = [(sE, u_done[gpar])]
                um = 0
                for hc in range(NHC):
                    um = p.do("tensor", lambda EE, h=hc, j=j4, sp=spar, gp=gpar,
                              bp=bpar:
                              EE.matmul(
                                  pu[gp][:, :],
                                  wu_view(sp)[:, h * 512 + j * 128: h * 512 + j * 128 + 128],
                                  xn_tb[bp][:, h * 512:(h + 1) * 512],
                                  start=(h == 0), stop=(h == NHC - 1)),
                              wait=wmm, inc=pe if hc == NHC - 1 else None) or um
                    wmm = []
                gu_last_mm = um
                if j4 == 3:
                    span_free[spar] = um
                sil_done[gpar] = p.do("scalar", lambda EE, gp=gpar, j=jc:
                                      EE.activation(
                                          silb[gp][:, :], pg[gp][:, :], AF.Silu,
                                          scale=sg_sb[:, j:j + 1]),
                                      wait=[(pe, gm), (vE, fuse_written[jc - 2] if jc >= 2 else 0)],
                                      inc=sE)
                u_done[gpar] = p.do("scalar", lambda EE, gp=gpar, j=jc:
                                    EE.activation(
                                        usb[gp][:, :], pu[gp][:, :], AF.Identity,
                                        scale=su_sb[:, j:j + 1]),
                                    wait=[(pe, um), (vE, fuse_written[jc - 2] if jc >= 2 else 0)],
                                    inc=sE)
                fuse_written[jc] = p.do("vector", lambda EE, gp=gpar, j=jc:
                                        EE.tensor_tensor(
                                            fuse[:, j * 512:(j + 1) * 512],
                                            usb[gp][:, :], silb[gp][:, :], ALU.mult),
                                        wait=[(sE, sil_done[gpar]), (sE, u_done[gpar]),
                                              (pe, dn_last_mm if b > 0 else 0)],
                                        inc=vE)
        fuse_all = fuse_written[NJC - 1]
        def load_wd(mch_):
            wp = mch_ % 2
            w = [(pe, gu_last_mm)] if mch_ < 2 else [(pe, dn_half_mm[wp])]
            wd_dma[wp] = p.dma(
                "sync", wbig[:, wp * 32 * 512:(wp + 1) * 32 * 512],
                wd_bf[mch_], dH, wait=w)

        load_wd(0)
        for mch in range(8):
            wpar = mch % 2
            if mch + 1 < 8:
                load_wd(mch + 1)
            for tt in range(4):
                dpar = (mch * 4 + tt) % 4
                ppd = tt % 2
                row0 = tb * 512 + tt * 128
                x_dma[dpar] = p.dma("sync", xtile[dpar][:, 0:256],
                                    x_in[e, row0:row0 + 128, mch * 256:(mch + 1) * 256],
                                    dH, wait=[(gE, gadd_done[dpar])])
                pr_dma[dpar] = p.dma("sync", prtile[dpar][:, 0:256],
                                     proj_scr[e, row0:row0 + 128, mch * 256:(mch + 1) * 256],
                                     dH, wait=[(vE, ev_last_cb[dpar])])
                w = [(dH, wd_dma[wpar]), (vE, fuse_all), (vE, pd_free[ppd])]
                dm = 0
                for jc in range(NJC):
                    dm = p.do("tensor", lambda EE, j=jc, t=tt, dp=ppd, wp=wpar:
                              EE.matmul(
                                  pd[dp][:, 0:256],
                                  fuse[:, j * 512 + t * 128: j * 512 + t * 128 + 128],
                                  wbig[:, wp * 32 * 512 + j * 256:
                                       wp * 32 * 512 + (j + 1) * 256],
                                  start=(j == 0), stop=(j == NJC - 1)), wait=w,
                              inc=pe if jc == NJC - 1 else None) or dm
                    w = []
                if tt == 3:
                    dn_half_mm[wpar] = dm
                dn_last_mm = dm
                ca = p.do("vector", lambda EE, dp=dpar, pp=ppd, m=mch:
                    EE.tensor_tensor(
                    dtile[dp][:, 0:256], pd[pp][:, 0:256],
                    sdb_sb[:, m * 256:(m + 1) * 256], ALU.mult),
                    wait=[(pe, dm), (gE, gadd_done[dpar])], inc=vE)
                pd_free[ppd] = ca
                cb = p.do("vector", lambda EE, dp=dpar: EE.tensor_tensor(
                    xtile[dp][:, 0:256], xtile[dp][:, 0:256],
                    prtile[dp][:, 0:256], ALU.add),
                    wait=[(dH, x_dma[dpar]), (dH, pr_dma[dpar]),
                          (gE, gadd_done[dpar])], inc=vE)
                ev_last_cb[dpar] = cb
                gadd_done[dpar] = p.do(
                    "gpsimd", lambda EE, dp=dpar: EE.tensor_tensor(
                        otile[dp][:, 0:256], dtile[dp][:, 0:256],
                        xtile[dp][:, 0:256], ALU.add),
                    wait=[(vE, ca), (vE, cb), (dH, o_dma[dpar])], inc=gE)
                ev_last[dpar] = gadd_done[dpar]
                o_dma[dpar] = p.dma(
                    "sync", out[e, row0:row0 + 128, mch * 256:(mch + 1) * 256],
                    otile[dpar][:, 0:256], dH, wait=[(gE, gadd_done[dpar])])
        sc5.close()

    p.wait("sync", dH, dH.n)
    p.wait("gpsimd", dS, dS.n)
    p.wait("scalar", dA, dA.n)
    barrier()
    for s_ in all_sems:
        p.do("sync", lambda EE, hh=s_.h: EE.sem_clear(hh))

    p.emit(nc)
    es.close()
    return nc



def _host_constants():
    ident = np.eye(128, dtype=BF)
    r2 = np.zeros((128, 128), dtype=np.float32)
    for base in (0, 64):
        for i in range(32):
            r2[base + i, base + i + 32] = -1.0
            r2[base + 32 + i, base + i] = 1.0
    r2t = np.ascontiguousarray(r2.T).astype(BF)
    inv_freq = (1.0 / (10000.0 ** (np.arange(0, HD, 2, dtype=np.float32) / HD))
                ).astype(np.float32)
    pos = np.arange(S, dtype=np.float32)
    freqs = pos[:, None] * inv_freq[None, :]
    emb = np.concatenate([freqs, freqs], axis=1)
    cos = np.cos(emb).astype(np.float32).T
    sin = np.sin(emb).astype(np.float32).T
    cosT = np.ascontiguousarray(np.concatenate([cos, cos], axis=0))
    sinT = np.ascontiguousarray(np.concatenate([sin, sin], axis=0))
    kk = np.arange(128)[:, None]
    cc = np.arange(1024)[None, :]
    stair = ((cc - 512) >= kk).astype(BF)
    onesr = np.ones((1, 128), dtype=BF)
    onesc = np.zeros((128, 128), dtype=BF)
    onesc[:, 0] = 1
    return dict(ident=ident, r2t=r2t, cost=cosT, sint=sinT, stair=stair,
                onesr=onesr, onesc=onesc)


def kernel(L0_attn_hidden_states, L0_attn_Wq, L0_attn_Wk, L0_attn_Wv,
           L0_attn_Wo, L0_ffn_w_gate, L0_ffn_w_up, L0_ffn_w_down,
           L0_ffn_s_gate, L0_ffn_s_up, L0_ffn_s_down):
    x = np.asarray(L0_attn_hidden_states)
    wq = np.asarray(L0_attn_Wq)
    wk = np.asarray(L0_attn_Wk)
    wv = np.asarray(L0_attn_Wv)
    wo = np.asarray(L0_attn_Wo)
    wg = np.ascontiguousarray(np.asarray(L0_ffn_w_gate))
    wu = np.ascontiguousarray(np.asarray(L0_ffn_w_up))
    wd = np.ascontiguousarray(np.asarray(L0_ffn_w_down))
    sg = np.asarray(L0_ffn_s_gate)
    su = np.asarray(L0_ffn_s_up)
    sd = np.asarray(L0_ffn_s_down)

    perm = _qhead_perm()
    wqt = np.ascontiguousarray(wq.T[:, perm])
    wkt = np.ascontiguousarray(wk.T)
    wvt = np.ascontiguousarray(wv.T)
    wot = np.ascontiguousarray(wo.T[perm, :])
    sg_h = np.ascontiguousarray(sg.reshape(NJC, 128).T)
    su_h = np.ascontiguousarray(su.reshape(NJC, 128).T)
    sdb = np.ascontiguousarray(np.broadcast_to(sd[None, :], (128, D)))

    consts = _host_constants()
    nc = build_nc()

    base = dict(wqt=wqt, wkt=wkt, wvt=wvt, wot=wot, wg=wg, wu=wu, wd=wd,
                sg=sg_h, su=su_h, sdb=sdb, **consts)
    in_maps = []
    for core in range(NCORES):
        m = dict(base)
        m["x"] = x[core * E:(core + 1) * E]
        in_maps.append(m)

    res = run_bass_kernel_spmd(nc, in_maps, core_ids=list(range(NCORES)))
    outs = [np.asarray(r["out"]) for r in res.results]
    return np.concatenate(outs, axis=0)


# version 19
# speedup vs baseline: 1.0797x; 1.0797x over previous
import numpy as np
import ml_dtypes
from contextlib import ExitStack
from collections import deque

import concourse.bass as bass
import concourse.mybir as mybir
from concourse.bass_utils import run_bass_kernel_spmd

F32 = mybir.dt.float32
BF16 = mybir.dt.bfloat16
I8 = mybir.dt.int8
AF = mybir.ActivationFunctionType
ALU = mybir.AluOpType
AX = mybir.AxisListType
BF = ml_dtypes.bfloat16

NCORES = 8
E = 2
S = 2048
D = 2048
H = 32
KVH = 8
HD = 64
J = 8192
NT = S // 128
NHC = D // 128
NJC = J // 128
QC = 512
NQC = S // QC
TB = 512
NBLK = E * S // TB
EPS = 1e-5

ENGS = ("tensor", "vector", "scalar", "gpsimd", "sync")

_EVEN = [h for h in range(H) if ((h // 4) % 2) == 0]
_ODD = [h for h in range(H) if ((h // 4) % 2) == 1]
CHUNK_OF = {}
OFF_OF = {}
for _c in range(NHC):
    CHUNK_OF[_EVEN[_c]] = _c
    OFF_OF[_EVEN[_c]] = 0
    CHUNK_OF[_ODD[_c]] = _c
    OFF_OF[_ODD[_c]] = 64


def _qhead_perm():
    perm = np.empty(D, dtype=np.int64)
    for c in range(NHC):
        for off, hh in ((0, _EVEN[c]), (64, _ODD[c])):
            for d in range(HD):
                perm[c * 128 + off + d] = hh * HD + d
    return perm


class Sem:
    def __init__(self, nc, es, name):
        self.h = es.enter_context(nc.semaphore(name))
        self.n = 0
        self.name = name


class Plan:
    def __init__(self):
        self.q = {e: [] for e in ENGS}
        self.last_wait = {e: {} for e in ENGS}

    def wait(self, eng, sem, val):
        if val is None or val <= 0:
            return
        lw = self.last_wait[eng]
        if lw.get(sem.name, 0) >= val:
            return
        lw[sem.name] = val
        self.q[eng].append(lambda EE, s=sem.h, v=val: EE.wait_ge(s, v))

    def do(self, eng, fn, wait=(), inc=None, inc_by=1):
        for (sem, val) in wait:
            self.wait(eng, sem, val)
        if inc is None:
            self.q[eng].append(fn)
            return None
        inc.n += inc_by
        self.q[eng].append(
            lambda EE, f=fn, s=inc.h, b=inc_by: f(EE).then_inc(s, b))
        return inc.n

    def dma(self, eng, out_ap, in_ap, sem, wait=()):
        return self.do(eng, lambda EE, o=out_ap, i=in_ap: EE.dma_start(o, i),
                       wait=wait, inc=sem, inc_by=16)

    def emit(self, nc):
        with nc.Block() as block:
            for eng in ENGS:
                def runner(EE, _eng=eng):
                    for f in self.q[_eng]:
                        f(EE)
                getattr(block, eng)(runner)


DEBUG_STAGE = None
DEBUG_QKV_LIMIT = None
DEBUG_NO_V = False
DEBUG_NO_ROPE = False


def build_nc():
    dbg = DEBUG_STAGE
    nc = bass.Bass("TRN2", target_bir_lowering=False)

    x_in = nc.dram_tensor("x", [E, S, D], F32, kind="ExternalInput")
    wqt = nc.dram_tensor("wqt", [D, D], F32, kind="ExternalInput")
    wkt_d = nc.dram_tensor("wkt", [D, KVH * HD], F32, kind="ExternalInput")
    wvt = nc.dram_tensor("wvt", [D, KVH * HD], F32, kind="ExternalInput")
    wot = nc.dram_tensor("wot", [D, D], F32, kind="ExternalInput")
    wg_d = nc.dram_tensor("wg", [D, J], I8, kind="ExternalInput")
    wu_d = nc.dram_tensor("wu", [D, J], I8, kind="ExternalInput")
    wd_d = nc.dram_tensor("wd", [J, D], I8, kind="ExternalInput")
    sg_in = nc.dram_tensor("sg", [128, NJC], F32, kind="ExternalInput")
    su_in = nc.dram_tensor("su", [128, NJC], F32, kind="ExternalInput")
    sdb_in = nc.dram_tensor("sdb", [128, D], F32, kind="ExternalInput")
    ident_in = nc.dram_tensor("ident", [128, 128], BF16, kind="ExternalInput")
    r2t_in = nc.dram_tensor("r2t", [128, 128], BF16, kind="ExternalInput")
    cost_in = nc.dram_tensor("cost", [128, S], F32, kind="ExternalInput")
    sint_in = nc.dram_tensor("sint", [128, S], F32, kind="ExternalInput")
    stair_in = nc.dram_tensor("stair", [128, 1024], BF16, kind="ExternalInput")
    onesr_in = nc.dram_tensor("onesr", [1, 128], BF16, kind="ExternalInput")
    onesc_in = nc.dram_tensor("onesc", [128, 128], BF16, kind="ExternalInput")

    out = nc.dram_tensor("out", [E, S, D], F32, kind="ExternalOutput")
    skind = dict(kind="ExternalOutput") if dbg else {}
    xnt_scr = nc.dram_tensor("xnt_scr", [E, NHC, 128, S], BF16, **skind)
    wg_bf = nc.dram_tensor("wg_bf", [16, 128, NHC * 512], BF16)
    wu_bf = nc.dram_tensor("wu_bf", [16, 128, NHC * 512], BF16)
    wd_bf = nc.dram_tensor("wd_bf", [4, 128, NJC * 512], BF16)
    wot_bf = nc.dram_tensor("wot_bf", [4, 128, NHC * 512], BF16)

    p = Plan()
    es = ExitStack()

    dH = Sem(nc, es, "dH")
    dS = Sem(nc, es, "dS")
    pe = Sem(nc, es, "pe")
    vE = Sem(nc, es, "vE")
    sE = Sem(nc, es, "sE")
    dn = Sem(nc, es, "dn")
    gE = Sem(nc, es, "gE")
    dA = Sem(nc, es, "dA")
    rS = [Sem(nc, es, f"rS{i}") for i in range(12)]
    all_sems = [dH, dS, pe, vE, sE, dn, gE, dA] + rS
    nbarrier = [0]

    def barrier():
        for eng in ENGS:
            dn.n += 1
            p.q[eng].append(lambda EE, s=dn.h: EE.drain().then_inc(s, 1))
        tgt = dn.n
        for eng in ENGS:
            p.wait(eng, dn, tgt)
        nbarrier[0] += 1

    cnegs = es.enter_context(nc.sbuf_tensor("cnegs_sb", [128, E * H], F32))
    sg_sb = es.enter_context(nc.sbuf_tensor("sg_sb", [128, NJC], F32))
    su_sb = es.enter_context(nc.sbuf_tensor("su_sb", [128, NJC], F32))
    p12c = ExitStack()
    ident = p12c.enter_context(nc.sbuf_tensor("ident_sb", [128, 128], BF16))
    r2t = p12c.enter_context(nc.sbuf_tensor("r2t_sb", [128, 128], BF16))
    stair = p12c.enter_context(nc.sbuf_tensor("stair_sb", [128, 1024], BF16))
    onesr = p12c.enter_context(nc.sbuf_tensor("onesr_sb", [1, 128], BF16))
    onesc = p12c.enter_context(nc.sbuf_tensor("onesc_sb", [128, 128], BF16))
    cosT = p12c.enter_context(nc.sbuf_tensor("cosT_sb", [128, S], F32))
    sinT = p12c.enter_context(nc.sbuf_tensor("sinT_sb", [128, S], F32))

    p.dma("sync", ident[:, :], ident_in[:, :], dH)
    p.dma("sync", r2t[:, :], r2t_in[:, :], dH)
    p.dma("sync", stair[:, :], stair_in[:, :], dH)
    p.dma("sync", onesr[0:1, :], onesr_in[0:1, :], dH)
    p.dma("sync", onesc[:, :], onesc_in[:, :], dH)
    p.dma("sync", sg_sb[:, :], sg_in[:, :], dH)
    p.dma("sync", su_sb[:, :], su_in[:, :], dH)
    p.dma("sync", cosT[:, :], cost_in[:, :], dH)
    const_dma = p.dma("sync", sinT[:, :], sint_in[:, :], dH)
    for _eng in ENGS:
        p.wait(_eng, dH, const_dma)

    ot_scr = nc.dram_tensor("ot_scr", [E, NQC, 128, NHC * QC], BF16, **skind)
    if dbg:
        qdbg = nc.dram_tensor("qdbg", [E, 128, NHC * S], BF16, kind="ExternalOutput")
        ptdbg = nc.dram_tensor("ptdbg", [2, 128, 1024], BF16, kind="ExternalOutput")
        podbg = nc.dram_tensor("podbg", [65, QC], F32, kind="ExternalOutput")
        kdbg = nc.dram_tensor("kdbg", [E, 128, 4 * S], BF16, kind="ExternalOutput")
        vdbg = nc.dram_tensor("vdbg", [E, 128, NT * 520], BF16, kind="ExternalOutput")
        cdbg = nc.dram_tensor("cdbg", [128, E * H], F32, kind="ExternalOutput")

    def snap_waits():
        return [(pe, pe.n), (vE, vE.n), (sE, sE.n)]

    conv_state = [0]

    def p1_p2_elem(e):
        cneg = cnegs[:, e * H:(e + 1) * H]
        scope = ExitStack()
        sxnt = ExitStack()
        xnt = sxnt.enter_context(nc.sbuf_tensor(f"xnt{e}", [128, NHC * S], BF16,
                                                side="right"))

        sc1 = ExitStack()
        xb = [sc1.enter_context(nc.sbuf_tensor(f"xb{e}{i}", [128, D], F32,
                                               side="right")) for i in range(2)]
        xsq = [sc1.enter_context(nc.sbuf_tensor(f"xsq{e}{i}", [128, D], F32,
                                                side="right")) for i in range(2)]
        st = sc1.enter_context(nc.sbuf_tensor(f"st{e}", [128, 24], F32,
                                              side="right"))
        xnb = [sc1.enter_context(nc.sbuf_tensor(f"xnb{e}{i}", [128, D], BF16,
                                                side="right")) for i in range(2)]
        ptrp = [sc1.enter_context(nc.psum_tensor(f"ptr{e}{i}", [128, D], F32))
                for i in range(2)]
        x_done = [0] * NT
        xn_done = [0] * NT
        tr_done = [0] * NT
        ev_done = [0] * NT
        sq_done = [0] * NT
        r2_done = [0] * NT
        ch_done = [0] * NT

        def ln_stats(tt):
            pr = tt % 2
            so = pr * 12
            c_r1 = p.do("vector", lambda EE, b=xb[pr], s=so: EE.reduce_sum(
                st[:, s:s + 1], b[:, :], axis=AX.X),
                wait=[(dH, x_done[tt])], inc=vE)
            w = [(dH, x_done[tt])]
            if tt >= 2:
                w.append((vE, r2_done[tt - 2]))
            c_sq = p.do("scalar", lambda EE, b=xb[pr], o=xsq[pr]: EE.activation(
                o[:, :], b[:, :], AF.Square), wait=w, inc=sE)
            sq_done[tt] = c_sq
            r2_done[tt] = p.do("vector", lambda EE, i_=xsq[pr], s=so:
                               EE.reduce_sum(
                st[:, s + 1:s + 2], i_[:, :], axis=AX.X),
                wait=[(sE, c_sq), (vE, c_r1)], inc=vE)

        def ln_chain(tt):
            pr = tt % 2
            so = pr * 12
            c = r2_done[tt]
            c = p.do("vector", lambda EE, s=so: EE.tensor_scalar(
                st[:, s + 2:s + 3], st[:, s:s + 1], 1.0 / D, None, ALU.mult),
                wait=[(vE, c)], inc=vE)
            c = p.do("vector", lambda EE, s=so: EE.tensor_scalar(
                st[:, s + 3:s + 4], st[:, s + 1:s + 2], 1.0 / D, EPS,
                ALU.mult, ALU.add), wait=[(vE, c)], inc=vE)
            c = p.do("vector", lambda EE, s=so: EE.tensor_tensor(
                st[:, s + 4:s + 5], st[:, s + 2:s + 3], st[:, s + 2:s + 3],
                ALU.mult), wait=[(vE, c)], inc=vE)
            c_var = p.do("vector", lambda EE, s=so: EE.tensor_tensor(
                st[:, s + 5:s + 6], st[:, s + 3:s + 4], st[:, s + 4:s + 5],
                ALU.subtract), wait=[(vE, c)], inc=vE)
            c_ln = p.do("scalar", lambda EE, s=so: EE.activation(
                st[:, s + 6:s + 7], st[:, s + 5:s + 6], AF.Ln),
                wait=[(vE, c_var)], inc=sE)
            c_rs = p.do("scalar", lambda EE, s=so: EE.activation(
                st[:, s + 9:s + 10], st[:, s + 6:s + 7], AF.Exp, scale=-0.5),
                wait=[(sE, c_ln)], inc=sE)
            ch_done[tt] = p.do("vector", lambda EE, s=so: EE.tensor_scalar(
                st[:, s + 10:s + 11], st[:, s + 2:s + 3], st[:, s + 9:s + 10],
                -1.0, ALU.mult, ALU.mult), wait=[(sE, c_rs)], inc=vE)

        def ln_xn(tt):
            pr = tt % 2
            so = pr * 12
            w = [(vE, ch_done[tt])]
            if tt >= 2:
                w.append((pe, tr_done[tt - 2]))
            xn_done[tt] = p.do("scalar", lambda EE, b=xb[pr], o=xnb[pr], s=so:
                               EE.activation(
                o[:, :], b[:, :], AF.Identity,
                scale=st[:, s + 9:s + 10], bias=st[:, s + 10:s + 11]),
                wait=w, inc=sE)

        def ln_tr(tt):
            pr = tt % 2
            w = [(sE, xn_done[tt])]
            if tt >= 2:
                w.append((vE, ev_done[tt - 2]))
            for hc in range(NHC):
                tr_done[tt] = p.do("tensor", lambda EE, h=hc, i_=xnb[pr],
                                   o=ptrp[pr]:
                                   EE.matmul(o[:, h * 128:(h + 1) * 128],
                                             i_[:, h * 128:(h + 1) * 128],
                                             ident[:, :], start=True, stop=True),
                                   wait=w, inc=pe if hc == NHC - 1 else None
                                   ) or tr_done[tt]
                w = []

        def ln_ev(tt):
            pr = tt % 2
            ev_done[tt] = p.do("vector", lambda EE, t=tt, o=ptrp[pr]:
                               EE.tensor_copy(
                xnt[:, :].rearrange("p (c t) -> p c t", t=S)[
                    :, :, t * 128:(t + 1) * 128],
                o[:, :].rearrange("p (c u) -> p c u", u=128)),
                wait=[(pe, tr_done[tt])], inc=vE)

        for tt in range(NT + 1):
            if tt < NT:
                pr = tt % 2
                w = [(dH, const_dma)]
                if tt >= 2:
                    w += [(sE, xn_done[tt - 2]), (vE, ev_done[tt - 2]),
                          (vE, r2_done[tt - 2])]
                if e > 0:
                    w += snap_prev
                x_done[tt] = p.dma("sync", xb[pr][:, :],
                                   x_in[e, tt * 128:(tt + 1) * 128, :], dH,
                                   wait=w)
                ln_stats(tt)
                ln_chain(tt)
                ln_xn(tt)
                ln_tr(tt)
            if tt >= 1:
                ln_ev(tt - 1)
        xnt_wr = 0
        for c in range(NHC):
            xnt_wr = p.dma("sync", xnt_scr[e, c], xnt[:, c * S:(c + 1) * S], dH,
                           wait=[(vE, ev_done[NT - 1])])
        sc1.close()
        if dbg == 1:
            sxnt.close()
            scope.close()
            return snap_waits() + [(dH, dH.n), (dS, dS.n)]
        qt = scope.enter_context(nc.sbuf_tensor(f"qt{e}", [128, NHC * S], BF16))
        kt4 = scope.enter_context(nc.sbuf_tensor(f"kt4{e}", [128, 4 * S], BF16))

        sc2 = ExitStack()
        wbuf = [sc2.enter_context(nc.sbuf_tensor(f"wqb{e}{i}", [128, NHC * 128],
                                                 BF16, side="right"))
                for i in range(2)]
        qbf = [sc2.enter_context(nc.sbuf_tensor(f"qbf{e}{i}", [128, S], BF16,
                                                side="right")) for i in range(2)]
        t1sb = sc2.enter_context(nc.sbuf_tensor(f"t1{e}", [128, 1024], F32,
                                                side="right"))
        t2sb = sc2.enter_context(nc.sbuf_tensor(f"t2{e}", [128, 1024], F32,
                                                side="right"))
        psA2 = [sc2.enter_context(nc.psum_tensor(f"pqA{e}{i}", [128, 1024], F32))
                for i in range(2)]
        psB2 = [sc2.enter_context(nc.psum_tensor(f"pqB{e}{i}", [128, 1024], F32))
                for i in range(2)]

        wsrcs = [(wqt, oc, qt, oc) for oc in range(NHC)] + \
                [(wkt_d, oc, kt4, oc) for oc in range(4)]
        if DEBUG_QKV_LIMIT is not None:
            wsrcs = wsrcs[:DEBUG_QKV_LIMIT]
        wdma = [0, 0]
        r2_read = [0, 0]
        qkv_done = [0, 0]
        qbf_done = [0, 0]
        t1_done = [0, 0]
        c_t2_done = [0, 0]
        rope_done = 0
        nhalf = 2 * len(wsrcs)

        def p1b_qkv(j):
            oc = j // 2
            par = oc % 2
            hp = j % 2
            g = (j % 2) * 1024
            wsrc, ocx, dstt, doc = wsrcs[oc]
            if j % 2 == 0:
                w = [(vE, ev_done[NT - 1])]
                if oc >= 2:
                    w.append((pe, qkv_done[par]))
                for hh in range(2):
                    wdma[par] = p.dma(
                        "gpsimd",
                        wbuf[par][:, hh * 8 * 128:(hh + 1) * 8 * 128].rearrange(
                            "p (c o) -> p c o", o=128),
                        wsrc.rearrange("(c p) o -> p c o", p=128)[
                            :, hh * 8:(hh + 1) * 8, ocx * 128:(ocx + 1) * 128],
                        dS, wait=w)
                    w = []
            w = [(dS, wdma[par]), (dH, xnt_wr),
                 (vE, t1_done[hp]), (sE, qbf_done[hp])]
            mm = 0
            for hc in range(NHC):
                for tc in range(2):
                    last = (hc == NHC - 1 and tc == 1)
                    mm = p.do("tensor", lambda EE, h=hc, t=tc, wb=wbuf[par],
                              g_=g, hpp=hp: EE.matmul(
                        psA2[hpp][:, t * 512:(t + 1) * 512],
                        wb[:, h * 128:(h + 1) * 128],
                        xnt[:, h * S + g_ + t * 512: h * S + g_ + (t + 1) * 512],
                        start=(h == 0), stop=(h == NHC - 1)), wait=w,
                        inc=pe if last else None) or mm
                    w = []
            qkv_done[par] = mm
            return mm

        def p1b_tail(j, mm_j):
            nonlocal rope_done
            oc = j // 2
            hp = j % 2
            g = (j % 2) * 1024
            wsrc, ocx, dstt, doc = wsrcs[oc]
            qbf_done[hp] = p.do("scalar", lambda EE, o=qbf[oc % 2], g_=g, hpp=hp:
                                EE.copy(o[:, g_:g_ + 1024], psA2[hpp][:, :]),
                                wait=[(pe, mm_j), (pe, r2_read[hp])], inc=sE)
            r2mm = 0
            w = [(sE, qbf_done[hp]), (vE, c_t2_done[hp])]
            for tc in range(2):
                r2mm = p.do("tensor", lambda EE, t=tc, qb=qbf[oc % 2], g_=g,
                            hpp=hp: EE.matmul(
                    psB2[hpp][:, t * 512:(t + 1) * 512], r2t[:, :],
                    qb[:, g_ + t * 512: g_ + (t + 1) * 512],
                    start=True, stop=True),
                    wait=w, inc=pe if tc == 1 else None) or r2mm
                w = []
            r2_read[hp] = r2mm
            t1_done[hp] = p.do("vector", lambda EE, g_=g, hpp=hp: EE.tensor_tensor(
                t1sb[:, :], psA2[hpp][:, :], cosT[:, g_:g_ + 1024], ALU.mult),
                wait=[(pe, mm_j), (sE, qbf_done[hp])], inc=vE)
            c_t2 = 0
            for tcq in range(2):
                c_t2 = p.do("vector", lambda EE, t=tcq, g_=g, hpp=hp:
                            EE.tensor_tensor(
                    t2sb[:, t * 512:(t + 1) * 512],
                    psB2[hpp][:, t * 512:(t + 1) * 512],
                    sinT[:, g_ + t * 512: g_ + (t + 1) * 512], ALU.mult),
                    wait=[(pe, r2mm)], inc=vE)
            c_t2_done[hp] = c_t2
            for tcq in range(2):
                rope_done = p.do(
                    "vector", lambda EE, dt=dstt, o=doc, t=tcq, g_=g:
                    EE.tensor_tensor(
                        dt[:, o * S + g_ + t * 512: o * S + g_ + (t + 1) * 512],
                        t1sb[:, t * 512:(t + 1) * 512],
                        t2sb[:, t * 512:(t + 1) * 512], ALU.add),
                    wait=[(vE, c_t2)], inc=vE)

        mm_hist = {}
        for j in range(nhalf + 1):
            if j < nhalf:
                mm_hist[j] = p1b_qkv(j)
            if j >= 1:
                p1b_tail(j - 1, mm_hist[j - 1])
        sc2.close()

        v65 = scope.enter_context(nc.sbuf_tensor(f"v65_{e}", [128, NT * 520 + 64],
                                                 BF16))
        v_done = rope_done
        sc2b = ExitStack()
        wv_sb = sc2b.enter_context(nc.sbuf_tensor(f"wv{e}", [128, NHC * 512],
                                                  BF16, side="right"))
        pvA = sc2b.enter_context(nc.psum_tensor(f"pvA{e}", [128, 512], F32))
        pvB = sc2b.enter_context(nc.psum_tensor(f"pvB{e}", [128, 512], F32))
        wv_dma = 0
        w = [(pe, qkv_done[0]), (pe, qkv_done[1]),
             (pe, r2_read[0]), (pe, r2_read[1]),
             (sE, qbf_done[0]), (sE, qbf_done[1]),
             (vE, rope_done), (vE, t1_done[0]), (vE, t1_done[1]),
             (vE, c_t2_done[0]), (vE, c_t2_done[1])]
        for hh in range(2 if not DEBUG_NO_V else 0):
            wv_dma = p.dma(
                "gpsimd",
                wv_sb[:, hh * 8 * 512:(hh + 1) * 8 * 512].rearrange(
                    "p (c o) -> p c o", o=512),
                wvt.rearrange("(c p) o -> p c o", p=128)[:, hh * 8:(hh + 1) * 8, :],
                dS, wait=w)
            w = []
        if e == 0:
            for (src_d, dst_d) in ((wg_d, wg_bf), (wu_d, wu_bf)):
                for s_ in range(16):
                    conv_state[0] = p.dma(
                        "gpsimd",
                        dst_d[s_].rearrange("p (c j) -> p c j", j=512),
                        src_d.rearrange("(c p) (s j) -> s p c j", p=128, j=512)[s_],
                        dS)
            for m_ in range(4):
                conv_state[0] = p.dma(
                    "gpsimd",
                    wd_bf[m_].rearrange("p (c n) -> p c n", n=512),
                    wd_d.rearrange("(c p) (a n) -> a p c n", p=128, n=512)[m_],
                    dS)
            for m_ in range(4):
                conv_state[0] = p.dma(
                    "gpsimd",
                    wot_bf[m_].rearrange("p (c n) -> p c n", n=512),
                    wot.rearrange("(c p) (a n) -> a p c n", p=128, n=512)[m_],
                    dS)
        v_ev = [0] * NT
        for tt in range(NT if not DEBUG_NO_V else 0):
            pr = tt % 2
            psv = pvA if pr == 0 else pvB
            w = [(dS, wv_dma), (vE, rope_done),
                 (sE, qbf_done[0]), (sE, qbf_done[1])]
            if tt >= 2:
                w.append((vE, v_ev[tt - 2]))
            mm = 0
            for hc in range(NHC):
                mm = p.do("tensor", lambda EE, h=hc, t=tt, o=psv: EE.matmul(
                    o[:, 0:512],
                    xnt[:, h * S + t * 128: h * S + t * 128 + 128],
                    wv_sb[:, h * 512:(h + 1) * 512],
                    start=(h == 0), stop=(h == NHC - 1)), wait=w,
                    inc=pe if hc == NHC - 1 else None) or mm
                w = []
            c_ms = p.do("vector", lambda EE, t=tt: EE.memset(
                v65[:, t * 520:(t + 1) * 520], 1.0), wait=[(pe, mm)], inc=vE)
            p.wait("vector", vE, c_ms)
            v_ev[tt] = p.do("vector", lambda EE, t=tt, o=psv: EE.tensor_copy(
                v65[:, t * 520:(t + 1) * 520].rearrange(
                    "p (h j) -> p h j", j=65)[:, :, 0:64],
                o[:, 0:512].rearrange("p (h j) -> p h j", j=64)), inc=vE)
        v_done = v_ev[NT - 1] if not DEBUG_NO_V else rope_done
        sc2b.close()
        sxnt.close()
        if dbg == 2:
            p.dma("sync", qdbg[e], qt[:, :], dH, wait=[(vE, rope_done)])
            p.dma("sync", kdbg[e], kt4[:, :], dH)
            p.dma("sync", vdbg[e], v65[:, :], dH, wait=[(vE, v_done)])
            scope.close()
            return snap_waits() + [(dH, dH.n), (dS, dS.n)]

        sc3 = ExitStack()
        qsq = [sc3.enter_context(nc.sbuf_tensor(f"qsq{e}{i}", [128, S], BF16,
                                                side="right")) for i in range(2)]
        n2 = sc3.enter_context(nc.sbuf_tensor(f"n2{e}", [1, H + KVH + 1], F32,
                                              side="right"))
        negc = sc3.enter_context(nc.sbuf_tensor(f"negc{e}", [1, H], BF16,
                                                side="right"))
        with nc.psum_tensor(f"pb{e}0", [128, S], F32) as pbA, \
             nc.psum_tensor(f"pb{e}1", [128, S], F32) as pbB:
            pb = [pbA, pbB]
            pb_free = [0, 0]
            sq_used = 0
            last_red = 0
            chunk_list = [(qt, c, (0, _EVEN[c]), (64, _ODD[c]))
                          for c in range(NHC)] + \
                         [(kt4, c, (0, H + 2 * c), (64, H + 2 * c + 1))
                          for c in range(4)]
            chunks = chunk_list
            sq_done_c = {}
            sq_used_p = [0, 0]
            for ci, (srct, c, ha, hb) in enumerate(chunks):
                par = ci % 2
                sq_done_c[ci] = p.do("vector", lambda EE, s=srct, cc=c, pq=par:
                                     EE.tensor_tensor(
                    qsq[pq][:, :], s[:, cc * S:(cc + 1) * S],
                    s[:, cc * S:(cc + 1) * S], ALU.mult),
                    wait=[(vE, rope_done), (vE, v_done), (pe, sq_used_p[par])],
                    inc=vE)
                for hi, (off, col) in enumerate((ha, hb)):
                    hpar = (2 * ci + hi) % 2
                    mmb = 0
                    for tc in range(4):
                        mmb = p.do("tensor", lambda EE, o=off, t=tc,
                                   b=pb[hpar], pq=par: EE.matmul(
                                       b[0:128, t * 512:(t + 1) * 512],
                                       onesc[o:o + 64, 0:128],
                                       qsq[pq][o:o + 64, t * 512:(t + 1) * 512],
                                       start=True, stop=True),
                                   wait=([(vE, sq_done_c[ci]),
                                          (vE, pb_free[hpar])]
                                         if tc == 0 else []),
                                   inc=pe if tc == 3 else None) or mmb
                    sq_used_p[par] = mmb
                    pb_free[hpar] = last_red = p.do(
                        "vector", lambda EE, cl=col, b=pb[hpar]: EE.reduce_max(
                            n2[0:1, cl:cl + 1], b[0:1, :], axis=AX.X),
                        wait=[(pe, mmb)], inc=vE)
            p.wait("vector", vE, last_red)
        c_c2 = 0
        for kv in range(KVH):
            c_c2 = p.do("vector", lambda EE, k=kv: EE.tensor_scalar(
                n2[0:1, 4 * k:4 * k + 4], n2[0:1, 4 * k:4 * k + 4],
                n2[0:1, H + k:H + k + 1], None, ALU.mult),
                wait=[(vE, last_red)], inc=vE)
        p.wait("vector", vE, c_c2)
        c_sqt = p.do("scalar", lambda EE: EE.activation(
            n2[0:1, 0:H], n2[0:1, 0:H], AF.Sqrt), wait=[(vE, c_c2)], inc=sE)
        c_ng = p.do("vector", lambda EE: EE.tensor_scalar(
            negc[0:1, :], n2[0:1, 0:H], -0.125, None, ALU.mult),
            wait=[(sE, c_sqt)], inc=vE)
        with nc.psum_tensor(f"pcn{e}", [128, H], F32) as pcn:
            c_bc = p.do("tensor", lambda EE: EE.matmul(
                pcn[:, :], onesr[0:1, :], negc[0:1, :], start=True, stop=True),
                wait=[(vE, c_ng)], inc=pe)
            c_cneg = p.do("vector", lambda EE: EE.tensor_copy(
                cneg[:, :], pcn[:, :]), wait=[(pe, c_bc)], inc=vE)
        sc3.close()
        if dbg is not None and dbg >= 3:
            p.dma("sync", cdbg[:, :], cnegs[:, :], dH, wait=[(vE, c_cneg)])
            p.dma("sync", qdbg[e], qt[:, :], dH)
            p.dma("sync", kdbg[e], kt4[:, :], dH)
            p.dma("sync", vdbg[e], v65[:, :], dH, wait=[(vE, v_done)])
        if dbg == 3:
            scope.close()
            return snap_waits() + [(dH, dH.n), (dS, dS.n)]

        sc4 = ExitStack()
        ktz = sc4.enter_context(nc.sbuf_tensor(f"ktz{e}", [128, KVH * S], BF16))
        kz_ms = p.do("vector", lambda EE: EE.memset(ktz[:, :], 0.0),
                     wait=[(vE, v_done), (vE, c_cneg)], inc=vE)
        p.wait("vector", vE, kz_ms)
        kz_done = 0
        for kvh_ in range(KVH):
            r0 = (kvh_ % 2) * 64
            kz_done = p.do("vector", lambda EE, kv=kvh_, r=r0: EE.tensor_copy(
                ktz[r:r + 64, kv * S:(kv + 1) * S],
                kt4[r:r + 64, (kv // 2) * S:(kv // 2 + 1) * S]), inc=vE)
        otq = sc4.enter_context(nc.sbuf_tensor(f"otq{e}", [128, NHC * QC], BF16,
                                               side="right"))
        ptb = [sc4.enter_context(nc.sbuf_tensor(f"pt{e}{i}", [128, 1024], BF16,
                                                side="right")) for i in range(4)]
        rsinv = sc4.enter_context(nc.sbuf_tensor(f"rsv{e}", [1, QC], BF16,
                                                 side="right"))
        invb = sc4.enter_context(nc.sbuf_tensor(f"invb{e}", [64, QC], F32,
                                                side="right"))
        pS = [sc4.enter_context(nc.psum_tensor(f"pS{e}{i}", [128, 1024], F32))
              for i in range(2)]
        pO = [sc4.enter_context(nc.psum_tensor(f"pO{e}{i}", [128, QC], F32))
              for i in range(2)]
        pI = sc4.enter_context(nc.psum_tensor(f"pI{e}", [128, QC], F32))

        exp_done = [0, 0]
        s_done = [0, 0]
        o_mm = [0, 0]
        o_mm4 = [0, 0, 0, 0]
        mask_done = [0, 0, 0, 0]
        ot_done = [0, 0]
        inv_mm = 0
        ib_done = 0
        ri_done = 0
        otq_dma = [0]

        steps = []
        for qc in range(NQC):
            nkt = 4 * qc + 4
            for h in range(H):
                for kp in range(nkt // 2):
                    steps.append((qc, h, kp, nkt))

        def emit_S(i):
            qc, h, kp, nkt = steps[i]
            kvh = h // 4
            qch, qoff = CHUNK_OF[h], OFF_OF[h]
            spar = i % 2
            w = [(vE, kz_done), (sE, exp_done[spar]), (vE, v_done)]
            s_mm = 0
            for half in range(2):
                ki = kp * 2 + half
                s_mm = p.do("tensor", lambda EE, k=ki, hf=half, sp=spar,
                            kv=kvh, qh_=qch, qq=qc: EE.matmul(
                    pS[sp][:, hf * 512:(hf + 1) * 512],
                    ktz[:, kv * S + k * 128: kv * S + k * 128 + 128],
                    qt[:, qh_ * S + qq * QC: qh_ * S + (qq + 1) * QC],
                    start=True, stop=True), wait=w,
                    inc=pe if half == 1 else None) or s_mm
                w = []
            s_done[spar] = s_mm

        def emit_tail(i):
            qc, h, kp, nkt = steps[i]
            kvh = h // 4
            hpar = h % 2
            spar = i % 2
            ptp = i % 4
            exp_done[spar] = p.do("scalar", lambda EE, sp=spar, pt=ptp, hh=h:
                                  EE.activation(
                                      ptb[pt][:, :], pS[sp][:, :], AF.Exp,
                                      scale=0.125, bias=cneg[:, hh:hh + 1]),
                                  wait=[(pe, s_done[spar]), (pe, o_mm4[ptp])],
                                  inc=sE)
            vv = 0
            for half in range(2):
                di = kp * 2 + half - 4 * qc
                if di >= 0:
                    nc_ = 128 * (di + 1)
                    vv = p.do("vector", lambda EE, pt=ptp, hf=half, d=di,
                              n_=nc_: EE.tensor_tensor(
                                  ptb[pt][:, hf * 512: hf * 512 + n_],
                                  ptb[pt][:, hf * 512: hf * 512 + n_],
                                  stair[:, 512 - 128 * d: 512 - 128 * d + n_],
                                  ALU.mult),
                              wait=[(sE, exp_done[spar])], inc=vE)
            if vv:
                mask_done[ptp] = vv
            w = [(sE, exp_done[spar])]
            if vv:
                w.append((vE, vv))
            if kp == 0:
                w.append((vE, ot_done[hpar]))
            for half in range(2):
                ki = kp * 2 + half
                o_mm4[ptp] = p.do("tensor", lambda EE, k=ki, hp=hpar, pt=ptp,
                                  hf=half, kv=kvh, nk=nkt: EE.matmul(
                                      pO[hp][:, :],
                                      v65[:, k * 520 + kv * 65: k * 520 + kv * 65 + 128],
                                      ptb[pt][:, hf * 512:(hf + 1) * 512],
                                      start=(k == 0), stop=(k == nk - 1)),
                                  wait=w, inc=pe if half == 1 else None
                                  ) or o_mm4[ptp]
                w = []
            o_mm[spar] = o_mm4[ptp]

        def emit_recip(i):
            nonlocal ri_done
            qc, h, kp, nkt = steps[i]
            hpar = h % 2
            def _recip(EE, hp=hpar):
                with nc.allow_low_precision("bf16 inv-rowsum operand for "
                                            "K=1 broadcast matmul"):
                    return EE.reciprocal(rsinv[0:1, :], pO[hp][64:65, :])
            ri_done = p.do("vector", _recip,
                           wait=[(pe, o_mm[0]), (pe, o_mm[1]),
                                 (pe, inv_mm)], inc=vE)

        def emit_norm(i):
            nonlocal inv_mm, ib_done
            qc, h, kp, nkt = steps[i]
            qch, qoff = CHUNK_OF[h], OFF_OF[h]
            hpar = h % 2
            inv_mm = p.do("tensor", lambda EE: EE.matmul(
                pI[:, :], onesr[0:1, 0:128], rsinv[0:1, :], start=True, stop=True),
                wait=[(vE, ri_done), (vE, ib_done)], inc=pe)
            for half in range(2):
                ib_done = p.do("vector", lambda EE, hf=half: EE.tensor_copy(
                    invb[:, hf * 256:(hf + 1) * 256],
                    pI[0:64, hf * 256:(hf + 1) * 256]),
                    wait=[(pe, inv_mm)], inc=vE)
            ww = [(dH, otq_dma[0]), (vE, ib_done)]
            for half in range(2):
                ot_done[hpar] = p.do(
                    "vector", lambda EE, hp=hpar, qh=qch, qo=qoff, hf=half:
                    EE.tensor_tensor(
                        otq[qo:qo + 64, qh * QC + hf * 256: qh * QC + (hf + 1) * 256],
                        pO[hp][0:64, hf * 256:(hf + 1) * 256],
                        invb[:, hf * 256:(hf + 1) * 256], ALU.mult),
                    wait=ww, inc=vE)
                ww = []

        nsteps = len(steps)
        for i in range(nsteps + 2):
            if i < nsteps:
                emit_S(i)
            if i >= 1 and i - 1 < nsteps:
                emit_tail(i - 1)
                qc, h, kp, nkt = steps[i - 1]
                if kp == nkt // 2 - 1:
                    emit_recip(i - 1)
            if i >= 2 and i - 2 < nsteps:
                j = i - 2
                qc, h, kp, nkt = steps[j]
                if kp == nkt // 2 - 1:
                    emit_norm(j)
                    if h == H - 1:
                        ot_all = max(ot_done)
                        otq_dma[0] = p.dma("sync", ot_scr[e, qc], otq[:, :],
                                           dH, wait=[(vE, ot_all)])
        sc4.close()
        scope.close()
        return snap_waits() + [(dH, dH.n), (dS, dS.n)]

    snap_prev = []
    for e in range(E):
        snap_prev = p1_p2_elem(e)
    barrier()
    if dbg is not None and dbg < 6:
        p.wait("sync", dH, dH.n)
        p.wait("gpsimd", dS, dS.n)
        p.wait("scalar", dA, dA.n)
        for s_ in all_sems:
            p.do("sync", lambda EE, hh=s_.h: EE.sem_clear(hh))
        p.emit(nc)
        es.close()
        return nc

    p12c.close()
    NR = 12
    PRE = 10
    sc5 = ExitStack()
    fuse = sc5.enter_context(nc.sbuf_tensor("fuse", [128, NJC * 512], BF16))
    sdb_sb = sc5.enter_context(nc.sbuf_tensor("sdb_sb", [128, D], F32))
    p.dma("sync", sdb_sb[:, :], sdb_in[:, :], dH)
    wgu = sc5.enter_context(nc.sbuf_tensor("wgu", [128, NJC * 512], BF16))
    xn_tb = [sc5.enter_context(nc.sbuf_tensor("xntb0", [128, NHC * 512], BF16))]
    xn_tb.append(xn_tb[0])
    otb = sc5.enter_context(nc.sbuf_tensor("otb", [128, NHC * 512], BF16))
    ring = sc5.enter_context(nc.sbuf_tensor("ring", [128, NR * 512], BF16))
    usb = [sc5.enter_context(nc.sbuf_tensor(f"usb{i}", [128, 512], F32))
           for i in range(2)]
    silb = [sc5.enter_context(nc.sbuf_tensor(f"silb{i}", [128, 512], F32))
            for i in range(2)]
    xtile = [sc5.enter_context(nc.sbuf_tensor(f"xt3{i}", [128, 512], F32))
             for i in range(4)]
    dtile = [sc5.enter_context(nc.sbuf_tensor(f"dt3{i}", [128, 512], F32))
             for i in range(4)]
    pg = [sc5.enter_context(nc.psum_tensor(f"pg{i}", [128, 512], F32))
          for i in range(2)]
    pu = [sc5.enter_context(nc.psum_tensor(f"pu{i}", [128, 512], F32))
          for i in range(2)]
    pj4 = [sc5.enter_context(nc.psum_tensor(f"pj{i}", [128, 512], F32))
           for i in range(4)]
    pd4 = [pg[0], pg[1], pu[0], pu[1]]

    def wg_view(par):
        return wgu[:, par * 16 * 512:(par + 1) * 16 * 512]

    def wu_view(par):
        return wgu[:, (2 + par) * 16 * 512:(3 + par) * 16 * 512]

    xn_dma = [0, 0]
    span_dma = [0, 0]
    span_free = [0, 0]
    sil_done = [0, 0]
    u_done = [0, 0]
    fuse_written = [0] * NJC
    gu_last_mm = 0
    fuse_free = 0
    pj_last = 0
    pd_free = 0
    pj_free = 0
    ca_tt = [0, 0, 0, 0]
    cb_tt = [0, 0, 0, 0]
    gadd_tt = [0, 0, 0, 0]
    o_dma_tt = [0, 0, 0, 0]
    x_dma_tt = [0, 0, 0, 0]
    dstop = [0, 0, 0, 0]
    jstop = [0, 0, 0, 0]
    slot_user = [0] * NR
    g_k = [0]
    inflight = deque()

    def emit_load(spec, b_):
        a_, kind, c_ = spec
        slot = g_k[0] % NR
        g_k[0] += 1
        src = (wd_bf[a_][:, c_ * 512:(c_ + 1) * 512] if kind == 0
               else wot_bf[a_][:, c_ * 512:(c_ + 1) * 512])
        w = [(pe, slot_user[slot])]
        if b_ == 0:
            w.append((dS, conv_state[0]))
        dv = p.dma("sync", ring[:, slot * 512:(slot + 1) * 512], src,
                   rS[slot], wait=w)
        inflight.append((slot, dv))

    for b in range(NBLK):
        e = b // (NBLK // E)
        tb = b % (NBLK // E)
        bpar = b % 2
        w = [(pe, gu_last_mm)] if b >= 1 else []
        for hh in range(2):
            xn_dma[bpar] = p.dma(
                "sync",
                xn_tb[bpar][:, hh * 8 * 512:(hh + 1) * 8 * 512].rearrange(
                    "p (c t) -> p c t", t=512),
                xnt_scr[e].rearrange("c p t -> p c t")[
                    :, hh * 8:(hh + 1) * 8, tb * 512:(tb + 1) * 512],
                dH, wait=w)
            w = []
        otb_dma = p.dma("sync", otb[:, :], ot_scr[e, tb], dH,
                        wait=[(pe, pj_last)] if b >= 1 else [])
        loads = deque()
        for a_ in range(4):
            for c_ in range(NJC):
                loads.append((a_, 0, c_))
            for c_ in range(NHC):
                loads.append((a_, 1, c_))
        for _ in range(PRE):
            emit_load(loads.popleft(), b)
        for s_ in range(16):
            spar = s_ % 2
            w = [(pe, span_free[spar]), (dS, conv_state[0])]
            for (wdram, view) in ((wg_bf, wg_view(spar)), (wu_bf, wu_view(spar))):
                span_dma[spar] = p.dma("sync", view[:, :], wdram[s_], dH, wait=w)
                w = []
            for j4 in range(4):
                jc = s_ * 4 + j4
                gpar = jc % 2
                wmm = [(dH, span_dma[spar]), (dH, xn_dma[bpar]),
                       (sE, sil_done[gpar])]
                if s_ == 0:
                    wmm.append((vE, ca_tt[gpar]))
                gm = 0
                for hc in range(NHC):
                    gm = p.do("tensor", lambda EE, h=hc, j=j4, sp=spar, gp=gpar,
                              bp=bpar:
                              EE.matmul(
                                  pg[gp][:, :],
                                  wg_view(sp)[:, h * 512 + j * 128: h * 512 + j * 128 + 128],
                                  xn_tb[bp][:, h * 512:(h + 1) * 512],
                                  start=(h == 0), stop=(h == NHC - 1)),
                              wait=wmm, inc=pe if hc == NHC - 1 else None) or gm
                    wmm = []
                wmm = [(sE, u_done[gpar])]
                if s_ == 0:
                    wmm.append((vE, ca_tt[2 + gpar]))
                um = 0
                for hc in range(NHC):
                    um = p.do("tensor", lambda EE, h=hc, j=j4, sp=spar, gp=gpar,
                              bp=bpar:
                              EE.matmul(
                                  pu[gp][:, :],
                                  wu_view(sp)[:, h * 512 + j * 128: h * 512 + j * 128 + 128],
                                  xn_tb[bp][:, h * 512:(h + 1) * 512],
                                  start=(h == 0), stop=(h == NHC - 1)),
                              wait=wmm, inc=pe if hc == NHC - 1 else None) or um
                    wmm = []
                gu_last_mm = um
                if j4 == 3:
                    span_free[spar] = um
                sil_done[gpar] = p.do("scalar", lambda EE, gp=gpar, j=jc:
                                      EE.activation(
                                          silb[gp][:, :], pg[gp][:, :], AF.Silu,
                                          scale=sg_sb[:, j:j + 1]),
                                      wait=[(pe, gm), (vE, fuse_written[jc - 2] if jc >= 2 else 0)],
                                      inc=sE)
                u_done[gpar] = p.do("scalar", lambda EE, gp=gpar, j=jc:
                                    EE.activation(
                                        usb[gp][:, :], pu[gp][:, :], AF.Identity,
                                        scale=su_sb[:, j:j + 1]),
                                    wait=[(pe, um), (vE, fuse_written[jc - 2] if jc >= 2 else 0)],
                                    inc=sE)
                fuse_written[jc] = p.do("vector", lambda EE, gp=gpar, j=jc:
                                        EE.tensor_tensor(
                                            fuse[:, j * 512:(j + 1) * 512],
                                            usb[gp][:, :], silb[gp][:, :], ALU.mult),
                                        wait=[(sE, sil_done[gpar]), (sE, u_done[gpar]),
                                              (pe, fuse_free if b > 0 else 0)],
                                        inc=vE)
        fuse_all = fuse_written[NJC - 1]
        for a in range(4):
            for c in range(NJC):
                slot, dv = inflight.popleft()
                w = [(rS[slot], dv)]
                if c == 0:
                    w += [(vE, fuse_all), (vE, pd_free)]
                last = 0
                fin = (c == NJC - 1)
                for tt in range(4):
                    last = p.do("tensor", lambda EE, sl=slot, t=tt, cc=c:
                                EE.matmul(
                                    pd4[t][:, :],
                                    fuse[:, cc * 512 + t * 128: cc * 512 + t * 128 + 128],
                                    ring[:, sl * 512:(sl + 1) * 512],
                                    start=(cc == 0), stop=(cc == NJC - 1)),
                                wait=w,
                                inc=pe if (fin or tt == 3) else None) or last
                    w = []
                    if fin:
                        dstop[tt] = last
                slot_user[slot] = last
                if loads:
                    emit_load(loads.popleft(), b)
            for tt in range(4):
                ca_tt[tt] = p.do("vector", lambda EE, t=tt, aa=a:
                                 EE.tensor_tensor(
                                     dtile[t][:, :], pd4[t][:, :],
                                     sdb_sb[:, aa * 512:(aa + 1) * 512],
                                     ALU.mult),
                                 wait=[(pe, dstop[tt]), (dH, o_dma_tt[tt])],
                                 inc=vE)
            pd_free = ca_tt[3]
            for oc in range(NHC):
                if oc % 4 == 0:
                    tx = oc // 4
                    x_dma_tt[tx] = p.dma(
                        "sync", xtile[tx][:, :],
                        x_in[e, tb * 512 + tx * 128: tb * 512 + tx * 128 + 128,
                             a * 512:(a + 1) * 512],
                        dH, wait=[(gE, gadd_tt[tx])])
                slot, dv = inflight.popleft()
                w = [(rS[slot], dv)]
                if oc == 0:
                    w += [(vE, pj_free), (dH, otb_dma)]
                last = 0
                fin = (oc == NHC - 1)
                for tt in range(4):
                    last = p.do("tensor", lambda EE, sl=slot, t=tt, oo=oc:
                                EE.matmul(
                                    pj4[t][:, :],
                                    otb[:, oo * 512 + t * 128: oo * 512 + t * 128 + 128],
                                    ring[:, sl * 512:(sl + 1) * 512],
                                    start=(oo == 0), stop=(oo == NHC - 1)),
                                wait=w,
                                inc=pe if (fin or tt == 3) else None) or last
                    w = []
                    if fin:
                        jstop[tt] = last
                slot_user[slot] = last
                if loads:
                    emit_load(loads.popleft(), b)
            pj_last = jstop[3]
            for tt in range(4):
                cb_tt[tt] = p.do("vector", lambda EE, t=tt: EE.tensor_tensor(
                    xtile[t][:, :], xtile[t][:, :], pj4[t][:, :], ALU.add),
                    wait=[(dH, x_dma_tt[tt]), (pe, jstop[tt])], inc=vE)
                gadd_tt[tt] = p.do("gpsimd", lambda EE, t=tt: EE.tensor_tensor(
                    dtile[t][:, :], dtile[t][:, :], xtile[t][:, :], ALU.add),
                    wait=[(vE, ca_tt[tt]), (vE, cb_tt[tt])], inc=gE)
                o_dma_tt[tt] = p.dma(
                    "sync",
                    out[e, tb * 512 + tt * 128: tb * 512 + tt * 128 + 128,
                        a * 512:(a + 1) * 512],
                    dtile[tt][:, :], dH, wait=[(gE, gadd_tt[tt])])
            pj_free = cb_tt[3]
        fuse_free = dstop[3]
    sc5.close()

    p.wait("sync", dH, dH.n)
    p.wait("gpsimd", dS, dS.n)
    p.wait("scalar", dA, dA.n)
    barrier()
    for s_ in all_sems:
        p.do("sync", lambda EE, hh=s_.h: EE.sem_clear(hh))

    p.emit(nc)
    es.close()
    return nc



def _host_constants():
    ident = np.eye(128, dtype=BF)
    r2 = np.zeros((128, 128), dtype=np.float32)
    for base in (0, 64):
        for i in range(32):
            r2[base + i, base + i + 32] = -1.0
            r2[base + 32 + i, base + i] = 1.0
    r2t = np.ascontiguousarray(r2.T).astype(BF)
    inv_freq = (1.0 / (10000.0 ** (np.arange(0, HD, 2, dtype=np.float32) / HD))
                ).astype(np.float32)
    pos = np.arange(S, dtype=np.float32)
    freqs = pos[:, None] * inv_freq[None, :]
    emb = np.concatenate([freqs, freqs], axis=1)
    cos = np.cos(emb).astype(np.float32).T
    sin = np.sin(emb).astype(np.float32).T
    cosT = np.ascontiguousarray(np.concatenate([cos, cos], axis=0))
    sinT = np.ascontiguousarray(np.concatenate([sin, sin], axis=0))
    kk = np.arange(128)[:, None]
    cc = np.arange(1024)[None, :]
    stair = ((cc - 512) >= kk).astype(BF)
    onesr = np.ones((1, 128), dtype=BF)
    onesc = np.zeros((128, 128), dtype=BF)
    onesc[:, 0] = 1
    return dict(ident=ident, r2t=r2t, cost=cosT, sint=sinT, stair=stair,
                onesr=onesr, onesc=onesc)


def kernel(L0_attn_hidden_states, L0_attn_Wq, L0_attn_Wk, L0_attn_Wv,
           L0_attn_Wo, L0_ffn_w_gate, L0_ffn_w_up, L0_ffn_w_down,
           L0_ffn_s_gate, L0_ffn_s_up, L0_ffn_s_down):
    x = np.asarray(L0_attn_hidden_states)
    wq = np.asarray(L0_attn_Wq)
    wk = np.asarray(L0_attn_Wk)
    wv = np.asarray(L0_attn_Wv)
    wo = np.asarray(L0_attn_Wo)
    wg = np.ascontiguousarray(np.asarray(L0_ffn_w_gate))
    wu = np.ascontiguousarray(np.asarray(L0_ffn_w_up))
    wd = np.ascontiguousarray(np.asarray(L0_ffn_w_down))
    sg = np.asarray(L0_ffn_s_gate)
    su = np.asarray(L0_ffn_s_up)
    sd = np.asarray(L0_ffn_s_down)

    perm = _qhead_perm()
    wqt = np.ascontiguousarray(wq.T[:, perm])
    wkt = np.ascontiguousarray(wk.T)
    wvt = np.ascontiguousarray(wv.T)
    wot = np.ascontiguousarray(wo.T[perm, :])
    sg_h = np.ascontiguousarray(sg.reshape(NJC, 128).T)
    su_h = np.ascontiguousarray(su.reshape(NJC, 128).T)
    sdb = np.ascontiguousarray(np.broadcast_to(sd[None, :], (128, D)))

    consts = _host_constants()
    nc = build_nc()

    base = dict(wqt=wqt, wkt=wkt, wvt=wvt, wot=wot, wg=wg, wu=wu, wd=wd,
                sg=sg_h, su=su_h, sdb=sdb, **consts)
    in_maps = []
    for core in range(NCORES):
        m = dict(base)
        m["x"] = x[core * E:(core + 1) * E]
        in_maps.append(m)

    res = run_bass_kernel_spmd(nc, in_maps, core_ids=list(range(NCORES)))
    outs = [np.asarray(r["out"]) for r in res.results]
    return np.concatenate(outs, axis=0)


# version 20
# speedup vs baseline: 1.1236x; 1.0407x over previous
import numpy as np
import ml_dtypes
from contextlib import ExitStack
from collections import deque

import concourse.bass as bass
import concourse.mybir as mybir
from concourse.bass_utils import run_bass_kernel_spmd

F32 = mybir.dt.float32
BF16 = mybir.dt.bfloat16
I8 = mybir.dt.int8
AF = mybir.ActivationFunctionType
ALU = mybir.AluOpType
AX = mybir.AxisListType
BF = ml_dtypes.bfloat16

NCORES = 8
E = 2
S = 2048
D = 2048
H = 32
KVH = 8
HD = 64
J = 8192
NT = S // 128
NHC = D // 128
NJC = J // 128
QC = 512
NQC = S // QC
TB = 512
NBLK = E * S // TB
EPS = 1e-5

ENGS = ("tensor", "vector", "scalar", "gpsimd", "sync")

_EVEN = [h for h in range(H) if ((h // 4) % 2) == 0]
_ODD = [h for h in range(H) if ((h // 4) % 2) == 1]
CHUNK_OF = {}
OFF_OF = {}
for _c in range(NHC):
    CHUNK_OF[_EVEN[_c]] = _c
    OFF_OF[_EVEN[_c]] = 0
    CHUNK_OF[_ODD[_c]] = _c
    OFF_OF[_ODD[_c]] = 64


def _qhead_perm():
    perm = np.empty(D, dtype=np.int64)
    for c in range(NHC):
        for off, hh in ((0, _EVEN[c]), (64, _ODD[c])):
            for d in range(HD):
                perm[c * 128 + off + d] = hh * HD + d
    return perm


class Sem:
    def __init__(self, nc, es, name):
        self.h = es.enter_context(nc.semaphore(name))
        self.n = 0
        self.name = name


class Plan:
    def __init__(self):
        self.q = {e: [] for e in ENGS}
        self.last_wait = {e: {} for e in ENGS}

    def wait(self, eng, sem, val):
        if val is None or val <= 0:
            return
        lw = self.last_wait[eng]
        if lw.get(sem.name, 0) >= val:
            return
        lw[sem.name] = val
        self.q[eng].append(lambda EE, s=sem.h, v=val: EE.wait_ge(s, v))

    def do(self, eng, fn, wait=(), inc=None, inc_by=1):
        for (sem, val) in wait:
            self.wait(eng, sem, val)
        if inc is None:
            self.q[eng].append(fn)
            return None
        inc.n += inc_by
        self.q[eng].append(
            lambda EE, f=fn, s=inc.h, b=inc_by: f(EE).then_inc(s, b))
        return inc.n

    def dma(self, eng, out_ap, in_ap, sem, wait=()):
        return self.do(eng, lambda EE, o=out_ap, i=in_ap: EE.dma_start(o, i),
                       wait=wait, inc=sem, inc_by=16)

    def emit(self, nc):
        with nc.Block() as block:
            for eng in ENGS:
                def runner(EE, _eng=eng):
                    for f in self.q[_eng]:
                        f(EE)
                getattr(block, eng)(runner)


DEBUG_STAGE = None
DEBUG_QKV_LIMIT = None
DEBUG_NO_V = False
DEBUG_NO_ROPE = False


def build_nc():
    dbg = DEBUG_STAGE
    nc = bass.Bass("TRN2", target_bir_lowering=False)

    x_in = nc.dram_tensor("x", [E, S, D], F32, kind="ExternalInput")
    wqt = nc.dram_tensor("wqt", [D, D], F32, kind="ExternalInput")
    wkt_d = nc.dram_tensor("wkt", [D, KVH * HD], F32, kind="ExternalInput")
    wvt = nc.dram_tensor("wvt", [D, KVH * HD], F32, kind="ExternalInput")
    wot = nc.dram_tensor("wot", [D, D], F32, kind="ExternalInput")
    wg_d = nc.dram_tensor("wg", [D, J], I8, kind="ExternalInput")
    wu_d = nc.dram_tensor("wu", [D, J], I8, kind="ExternalInput")
    wd_d = nc.dram_tensor("wd", [J, D], I8, kind="ExternalInput")
    sg_in = nc.dram_tensor("sg", [128, NJC], F32, kind="ExternalInput")
    su_in = nc.dram_tensor("su", [128, NJC], F32, kind="ExternalInput")
    sdb_in = nc.dram_tensor("sdb", [128, D], F32, kind="ExternalInput")
    ident_in = nc.dram_tensor("ident", [128, 128], BF16, kind="ExternalInput")
    r2t_in = nc.dram_tensor("r2t", [128, 128], BF16, kind="ExternalInput")
    cost_in = nc.dram_tensor("cost", [128, S], F32, kind="ExternalInput")
    sint_in = nc.dram_tensor("sint", [128, S], F32, kind="ExternalInput")
    stair_in = nc.dram_tensor("stair", [128, 1024], BF16, kind="ExternalInput")
    onesr_in = nc.dram_tensor("onesr", [1, 128], BF16, kind="ExternalInput")
    onesc_in = nc.dram_tensor("onesc", [128, 128], BF16, kind="ExternalInput")

    out = nc.dram_tensor("out", [E, S, D], F32, kind="ExternalOutput")
    skind = dict(kind="ExternalOutput") if dbg else {}
    xnt_scr = nc.dram_tensor("xnt_scr", [E, NHC, 128, S], BF16, **skind)
    wg_bf = nc.dram_tensor("wg_bf", [16, 128, NHC * 512], BF16)
    wu_bf = nc.dram_tensor("wu_bf", [16, 128, NHC * 512], BF16)
    wd_bf = nc.dram_tensor("wd_bf", [4, 128, NJC * 512], BF16)
    wot_bf = nc.dram_tensor("wot_bf", [4, 128, NHC * 512], BF16)

    p = Plan()
    es = ExitStack()

    dH = Sem(nc, es, "dH")
    dS = Sem(nc, es, "dS")
    pe = Sem(nc, es, "pe")
    vE = Sem(nc, es, "vE")
    sE = Sem(nc, es, "sE")
    dn = Sem(nc, es, "dn")
    gE = Sem(nc, es, "gE")
    dA = Sem(nc, es, "dA")
    rS = [Sem(nc, es, f"rS{i}") for i in range(12)]
    all_sems = [dH, dS, pe, vE, sE, dn, gE, dA] + rS
    nbarrier = [0]

    def barrier():
        for eng in ENGS:
            dn.n += 1
            p.q[eng].append(lambda EE, s=dn.h: EE.drain().then_inc(s, 1))
        tgt = dn.n
        for eng in ENGS:
            p.wait(eng, dn, tgt)
        nbarrier[0] += 1

    cnegs = es.enter_context(nc.sbuf_tensor("cnegs_sb", [128, E * H], F32))
    sg_sb = es.enter_context(nc.sbuf_tensor("sg_sb", [128, NJC], F32))
    su_sb = es.enter_context(nc.sbuf_tensor("su_sb", [128, NJC], F32))
    p12c = ExitStack()
    ident = p12c.enter_context(nc.sbuf_tensor("ident_sb", [128, 128], BF16))
    r2t = p12c.enter_context(nc.sbuf_tensor("r2t_sb", [128, 128], BF16))
    stair = p12c.enter_context(nc.sbuf_tensor("stair_sb", [128, 1024], BF16))
    onesr = p12c.enter_context(nc.sbuf_tensor("onesr_sb", [1, 128], BF16))
    onesc = p12c.enter_context(nc.sbuf_tensor("onesc_sb", [128, 128], BF16))
    cosT = p12c.enter_context(nc.sbuf_tensor("cosT_sb", [128, S], F32))
    sinT = p12c.enter_context(nc.sbuf_tensor("sinT_sb", [128, S], F32))

    p.dma("sync", ident[:, :], ident_in[:, :], dH)
    p.dma("sync", r2t[:, :], r2t_in[:, :], dH)
    p.dma("sync", stair[:, :], stair_in[:, :], dH)
    p.dma("sync", onesr[0:1, :], onesr_in[0:1, :], dH)
    p.dma("sync", onesc[:, :], onesc_in[:, :], dH)
    p.dma("sync", sg_sb[:, :], sg_in[:, :], dH)
    p.dma("sync", su_sb[:, :], su_in[:, :], dH)
    p.dma("sync", cosT[:, :], cost_in[:, :], dH)
    const_dma = p.dma("sync", sinT[:, :], sint_in[:, :], dH)
    for _eng in ENGS:
        p.wait(_eng, dH, const_dma)

    ot_scr = nc.dram_tensor("ot_scr", [E, NQC, 128, NHC * QC], BF16, **skind)
    if dbg:
        qdbg = nc.dram_tensor("qdbg", [E, 128, NHC * S], BF16, kind="ExternalOutput")
        ptdbg = nc.dram_tensor("ptdbg", [2, 128, 1024], BF16, kind="ExternalOutput")
        podbg = nc.dram_tensor("podbg", [65, QC], F32, kind="ExternalOutput")
        kdbg = nc.dram_tensor("kdbg", [E, 128, 4 * S], BF16, kind="ExternalOutput")
        vdbg = nc.dram_tensor("vdbg", [E, 128, NT * 520], BF16, kind="ExternalOutput")
        cdbg = nc.dram_tensor("cdbg", [128, E * H], F32, kind="ExternalOutput")

    def snap_waits():
        return [(pe, pe.n), (vE, vE.n), (sE, sE.n)]

    conv_state = [0]

    def p1_p2_elem(e):
        cneg = cnegs[:, e * H:(e + 1) * H]
        scope = ExitStack()
        sxnt = ExitStack()
        xnt = sxnt.enter_context(nc.sbuf_tensor(f"xnt{e}", [128, NHC * S], BF16,
                                                side="right"))

        sc1 = ExitStack()
        xb = [sc1.enter_context(nc.sbuf_tensor(f"xb{e}{i}", [128, D], F32,
                                               side="right")) for i in range(2)]
        xsq = [sc1.enter_context(nc.sbuf_tensor(f"xsq{e}{i}", [128, D], F32,
                                                side="right")) for i in range(2)]
        st = sc1.enter_context(nc.sbuf_tensor(f"st{e}", [128, 24], F32,
                                              side="right"))
        xnb = [sc1.enter_context(nc.sbuf_tensor(f"xnb{e}{i}", [128, D], BF16,
                                                side="right")) for i in range(2)]
        ptrp = [sc1.enter_context(nc.psum_tensor(f"ptr{e}{i}", [128, D], F32))
                for i in range(2)]
        x_done = [0] * NT
        xn_done = [0] * NT
        tr_done = [0] * NT
        ev_done = [0] * NT
        sq_done = [0] * NT
        r2_done = [0] * NT
        ch_done = [0] * NT

        def ln_stats(tt):
            pr = tt % 2
            so = pr * 12
            c_r1 = p.do("vector", lambda EE, b=xb[pr], s=so: EE.reduce_sum(
                st[:, s:s + 1], b[:, :], axis=AX.X),
                wait=[(dH, x_done[tt])], inc=vE)
            w = [(dH, x_done[tt])]
            if tt >= 2:
                w.append((vE, r2_done[tt - 2]))
            c_sq = p.do("scalar", lambda EE, b=xb[pr], o=xsq[pr]: EE.activation(
                o[:, :], b[:, :], AF.Square), wait=w, inc=sE)
            sq_done[tt] = c_sq
            r2_done[tt] = p.do("vector", lambda EE, i_=xsq[pr], s=so:
                               EE.reduce_sum(
                st[:, s + 1:s + 2], i_[:, :], axis=AX.X),
                wait=[(sE, c_sq), (vE, c_r1)], inc=vE)

        def ln_chain(tt):
            pr = tt % 2
            so = pr * 12
            c = r2_done[tt]
            c = p.do("vector", lambda EE, s=so: EE.tensor_scalar(
                st[:, s + 2:s + 3], st[:, s:s + 1], 1.0 / D, None, ALU.mult),
                wait=[(vE, c)], inc=vE)
            c = p.do("vector", lambda EE, s=so: EE.tensor_scalar(
                st[:, s + 3:s + 4], st[:, s + 1:s + 2], 1.0 / D, EPS,
                ALU.mult, ALU.add), wait=[(vE, c)], inc=vE)
            c = p.do("vector", lambda EE, s=so: EE.tensor_tensor(
                st[:, s + 4:s + 5], st[:, s + 2:s + 3], st[:, s + 2:s + 3],
                ALU.mult), wait=[(vE, c)], inc=vE)
            c_var = p.do("vector", lambda EE, s=so: EE.tensor_tensor(
                st[:, s + 5:s + 6], st[:, s + 3:s + 4], st[:, s + 4:s + 5],
                ALU.subtract), wait=[(vE, c)], inc=vE)
            c_ln = p.do("scalar", lambda EE, s=so: EE.activation(
                st[:, s + 6:s + 7], st[:, s + 5:s + 6], AF.Ln),
                wait=[(vE, c_var)], inc=sE)
            c_rs = p.do("scalar", lambda EE, s=so: EE.activation(
                st[:, s + 9:s + 10], st[:, s + 6:s + 7], AF.Exp, scale=-0.5),
                wait=[(sE, c_ln)], inc=sE)
            ch_done[tt] = p.do("vector", lambda EE, s=so: EE.tensor_scalar(
                st[:, s + 10:s + 11], st[:, s + 2:s + 3], st[:, s + 9:s + 10],
                -1.0, ALU.mult, ALU.mult), wait=[(sE, c_rs)], inc=vE)

        def ln_xn(tt):
            pr = tt % 2
            so = pr * 12
            w = [(vE, ch_done[tt])]
            if tt >= 2:
                w.append((pe, tr_done[tt - 2]))
            xn_done[tt] = p.do("scalar", lambda EE, b=xb[pr], o=xnb[pr], s=so:
                               EE.activation(
                o[:, :], b[:, :], AF.Identity,
                scale=st[:, s + 9:s + 10], bias=st[:, s + 10:s + 11]),
                wait=w, inc=sE)

        def ln_tr(tt):
            pr = tt % 2
            w = [(sE, xn_done[tt])]
            if tt >= 2:
                w.append((vE, ev_done[tt - 2]))
            for hc in range(NHC):
                tr_done[tt] = p.do("tensor", lambda EE, h=hc, i_=xnb[pr],
                                   o=ptrp[pr]:
                                   EE.matmul(o[:, h * 128:(h + 1) * 128],
                                             i_[:, h * 128:(h + 1) * 128],
                                             ident[:, :], start=True, stop=True),
                                   wait=w, inc=pe if hc == NHC - 1 else None
                                   ) or tr_done[tt]
                w = []

        def ln_ev(tt):
            pr = tt % 2
            ev_done[tt] = p.do("vector", lambda EE, t=tt, o=ptrp[pr]:
                               EE.tensor_copy(
                xnt[:, :].rearrange("p (c t) -> p c t", t=S)[
                    :, :, t * 128:(t + 1) * 128],
                o[:, :].rearrange("p (c u) -> p c u", u=128)),
                wait=[(pe, tr_done[tt])], inc=vE)

        for tt in range(NT + 1):
            if tt < NT:
                pr = tt % 2
                w = [(dH, const_dma)]
                if tt >= 2:
                    w += [(sE, xn_done[tt - 2]), (vE, ev_done[tt - 2]),
                          (vE, r2_done[tt - 2])]
                if e > 0:
                    w += snap_prev
                x_done[tt] = p.dma("sync", xb[pr][:, :],
                                   x_in[e, tt * 128:(tt + 1) * 128, :], dH,
                                   wait=w)
                ln_stats(tt)
                ln_chain(tt)
                ln_xn(tt)
                ln_tr(tt)
            if tt >= 1:
                ln_ev(tt - 1)
        xnt_wr = 0
        for c in range(NHC):
            xnt_wr = p.dma("sync", xnt_scr[e, c], xnt[:, c * S:(c + 1) * S], dH,
                           wait=[(vE, ev_done[NT - 1])])
        sc1.close()
        if dbg == 1:
            sxnt.close()
            scope.close()
            return snap_waits() + [(dH, dH.n), (dS, dS.n)]
        qt = scope.enter_context(nc.sbuf_tensor(f"qt{e}", [128, NHC * S], BF16))
        kt4 = scope.enter_context(nc.sbuf_tensor(f"kt4{e}", [128, 4 * S], BF16))

        sc2 = ExitStack()
        wbuf = [sc2.enter_context(nc.sbuf_tensor(f"wqb{e}{i}", [128, NHC * 128],
                                                 BF16, side="right"))
                for i in range(2)]
        qbf = [sc2.enter_context(nc.sbuf_tensor(f"qbf{e}{i}", [128, S], BF16,
                                                side="right")) for i in range(2)]
        t1sb = sc2.enter_context(nc.sbuf_tensor(f"t1{e}", [128, 1024], F32,
                                                side="right"))
        t2sb = sc2.enter_context(nc.sbuf_tensor(f"t2{e}", [128, 1024], F32,
                                                side="right"))
        psA2 = [sc2.enter_context(nc.psum_tensor(f"pqA{e}{i}", [128, 1024], F32))
                for i in range(2)]
        psB2 = [sc2.enter_context(nc.psum_tensor(f"pqB{e}{i}", [128, 1024], F32))
                for i in range(2)]

        wsrcs = [(wqt, oc, qt, oc) for oc in range(NHC)] + \
                [(wkt_d, oc, kt4, oc) for oc in range(4)]
        if DEBUG_QKV_LIMIT is not None:
            wsrcs = wsrcs[:DEBUG_QKV_LIMIT]
        wdma = [0, 0]
        r2_read = [0, 0]
        qkv_done = [0, 0]
        qbf_done = [0, 0]
        t1_done = [0, 0]
        c_t2_done = [0, 0]
        rope_done = 0
        nhalf = 2 * len(wsrcs)

        def p1b_qkv(j):
            oc = j // 2
            par = oc % 2
            hp = j % 2
            g = (j % 2) * 1024
            wsrc, ocx, dstt, doc = wsrcs[oc]
            if j % 2 == 0:
                w = [(vE, ev_done[NT - 1])]
                if oc >= 2:
                    w.append((pe, qkv_done[par]))
                for hh in range(2):
                    wdma[par] = p.dma(
                        "gpsimd",
                        wbuf[par][:, hh * 8 * 128:(hh + 1) * 8 * 128].rearrange(
                            "p (c o) -> p c o", o=128),
                        wsrc.rearrange("(c p) o -> p c o", p=128)[
                            :, hh * 8:(hh + 1) * 8, ocx * 128:(ocx + 1) * 128],
                        dS, wait=w)
                    w = []
            w = [(dS, wdma[par]), (dH, xnt_wr),
                 (vE, t1_done[hp]), (sE, qbf_done[hp])]
            mm = 0
            for hc in range(NHC):
                for tc in range(2):
                    last = (hc == NHC - 1 and tc == 1)
                    mm = p.do("tensor", lambda EE, h=hc, t=tc, wb=wbuf[par],
                              g_=g, hpp=hp: EE.matmul(
                        psA2[hpp][:, t * 512:(t + 1) * 512],
                        wb[:, h * 128:(h + 1) * 128],
                        xnt[:, h * S + g_ + t * 512: h * S + g_ + (t + 1) * 512],
                        start=(h == 0), stop=(h == NHC - 1)), wait=w,
                        inc=pe if last else None) or mm
                    w = []
            qkv_done[par] = mm
            return mm

        def p1b_tail(j, mm_j):
            nonlocal rope_done
            oc = j // 2
            hp = j % 2
            g = (j % 2) * 1024
            wsrc, ocx, dstt, doc = wsrcs[oc]
            qbf_done[hp] = p.do("scalar", lambda EE, o=qbf[oc % 2], g_=g, hpp=hp:
                                EE.copy(o[:, g_:g_ + 1024], psA2[hpp][:, :]),
                                wait=[(pe, mm_j), (pe, r2_read[hp])], inc=sE)
            r2mm = 0
            w = [(sE, qbf_done[hp]), (vE, c_t2_done[hp])]
            for tc in range(2):
                r2mm = p.do("tensor", lambda EE, t=tc, qb=qbf[oc % 2], g_=g,
                            hpp=hp: EE.matmul(
                    psB2[hpp][:, t * 512:(t + 1) * 512], r2t[:, :],
                    qb[:, g_ + t * 512: g_ + (t + 1) * 512],
                    start=True, stop=True),
                    wait=w, inc=pe if tc == 1 else None) or r2mm
                w = []
            r2_read[hp] = r2mm
            t1_done[hp] = p.do("vector", lambda EE, g_=g, hpp=hp: EE.tensor_tensor(
                t1sb[:, :], psA2[hpp][:, :], cosT[:, g_:g_ + 1024], ALU.mult),
                wait=[(pe, mm_j), (sE, qbf_done[hp])], inc=vE)
            c_t2 = 0
            for tcq in range(2):
                c_t2 = p.do("vector", lambda EE, t=tcq, g_=g, hpp=hp:
                            EE.tensor_tensor(
                    t2sb[:, t * 512:(t + 1) * 512],
                    psB2[hpp][:, t * 512:(t + 1) * 512],
                    sinT[:, g_ + t * 512: g_ + (t + 1) * 512], ALU.mult),
                    wait=[(pe, r2mm)], inc=vE)
            c_t2_done[hp] = c_t2
            for tcq in range(2):
                rope_done = p.do(
                    "vector", lambda EE, dt=dstt, o=doc, t=tcq, g_=g:
                    EE.tensor_tensor(
                        dt[:, o * S + g_ + t * 512: o * S + g_ + (t + 1) * 512],
                        t1sb[:, t * 512:(t + 1) * 512],
                        t2sb[:, t * 512:(t + 1) * 512], ALU.add),
                    wait=[(vE, c_t2)], inc=vE)

        mm_hist = {}
        for j in range(nhalf + 1):
            if j < nhalf:
                mm_hist[j] = p1b_qkv(j)
            if j >= 1:
                p1b_tail(j - 1, mm_hist[j - 1])
        sc2.close()

        v65 = scope.enter_context(nc.sbuf_tensor(f"v65_{e}", [128, NT * 520 + 64],
                                                 BF16))
        v_done = rope_done
        sc2b = ExitStack()
        wv_sb = sc2b.enter_context(nc.sbuf_tensor(f"wv{e}", [128, NHC * 512],
                                                  BF16, side="right"))
        pvA = sc2b.enter_context(nc.psum_tensor(f"pvA{e}", [128, 512], F32))
        pvB = sc2b.enter_context(nc.psum_tensor(f"pvB{e}", [128, 512], F32))
        wv_dma = 0
        w = [(pe, qkv_done[0]), (pe, qkv_done[1]),
             (pe, r2_read[0]), (pe, r2_read[1]),
             (sE, qbf_done[0]), (sE, qbf_done[1]),
             (vE, rope_done), (vE, t1_done[0]), (vE, t1_done[1]),
             (vE, c_t2_done[0]), (vE, c_t2_done[1])]
        for hh in range(2 if not DEBUG_NO_V else 0):
            wv_dma = p.dma(
                "gpsimd",
                wv_sb[:, hh * 8 * 512:(hh + 1) * 8 * 512].rearrange(
                    "p (c o) -> p c o", o=512),
                wvt.rearrange("(c p) o -> p c o", p=128)[:, hh * 8:(hh + 1) * 8, :],
                dS, wait=w)
            w = []
        if e == 0:
            for (src_d, dst_d) in ((wg_d, wg_bf), (wu_d, wu_bf)):
                for s_ in range(16):
                    conv_state[0] = p.dma(
                        "gpsimd",
                        dst_d[s_].rearrange("p (c j) -> p c j", j=512),
                        src_d.rearrange("(c p) (s j) -> s p c j", p=128, j=512)[s_],
                        dS)
            for m_ in range(4):
                conv_state[0] = p.dma(
                    "gpsimd",
                    wd_bf[m_].rearrange("p (c n) -> p c n", n=512),
                    wd_d.rearrange("(c p) (a n) -> a p c n", p=128, n=512)[m_],
                    dS)
            for m_ in range(4):
                conv_state[0] = p.dma(
                    "gpsimd",
                    wot_bf[m_].rearrange("p (c n) -> p c n", n=512),
                    wot.rearrange("(c p) (a n) -> a p c n", p=128, n=512)[m_],
                    dS)
        v_ev = [0] * NT
        for tt in range(NT if not DEBUG_NO_V else 0):
            pr = tt % 2
            psv = pvA if pr == 0 else pvB
            w = [(dS, wv_dma), (vE, rope_done),
                 (sE, qbf_done[0]), (sE, qbf_done[1])]
            if tt >= 2:
                w.append((vE, v_ev[tt - 2]))
            mm = 0
            for hc in range(NHC):
                mm = p.do("tensor", lambda EE, h=hc, t=tt, o=psv: EE.matmul(
                    o[:, 0:512],
                    xnt[:, h * S + t * 128: h * S + t * 128 + 128],
                    wv_sb[:, h * 512:(h + 1) * 512],
                    start=(h == 0), stop=(h == NHC - 1)), wait=w,
                    inc=pe if hc == NHC - 1 else None) or mm
                w = []
            c_ms = p.do("vector", lambda EE, t=tt: EE.memset(
                v65[:, t * 520:(t + 1) * 520], 1.0), wait=[(pe, mm)], inc=vE)
            p.wait("vector", vE, c_ms)
            v_ev[tt] = p.do("vector", lambda EE, t=tt, o=psv: EE.tensor_copy(
                v65[:, t * 520:(t + 1) * 520].rearrange(
                    "p (h j) -> p h j", j=65)[:, :, 0:64],
                o[:, 0:512].rearrange("p (h j) -> p h j", j=64)), inc=vE)
        v_done = v_ev[NT - 1] if not DEBUG_NO_V else rope_done
        sc2b.close()
        sxnt.close()
        if dbg == 2:
            p.dma("sync", qdbg[e], qt[:, :], dH, wait=[(vE, rope_done)])
            p.dma("sync", kdbg[e], kt4[:, :], dH)
            p.dma("sync", vdbg[e], v65[:, :], dH, wait=[(vE, v_done)])
            scope.close()
            return snap_waits() + [(dH, dH.n), (dS, dS.n)]

        sc3 = ExitStack()
        qsq = [sc3.enter_context(nc.sbuf_tensor(f"qsq{e}{i}", [128, S], BF16,
                                                side="right")) for i in range(2)]
        n2 = sc3.enter_context(nc.sbuf_tensor(f"n2{e}", [1, H + KVH + 1], F32,
                                              side="right"))
        negc = sc3.enter_context(nc.sbuf_tensor(f"negc{e}", [1, H], BF16,
                                                side="right"))
        with nc.psum_tensor(f"pb{e}0", [128, S], F32) as pbA, \
             nc.psum_tensor(f"pb{e}1", [128, S], F32) as pbB:
            pb = [pbA, pbB]
            pb_free = [0, 0]
            sq_used = 0
            last_red = 0
            chunk_list = [(qt, c, (0, _EVEN[c]), (64, _ODD[c]))
                          for c in range(NHC)] + \
                         [(kt4, c, (0, H + 2 * c), (64, H + 2 * c + 1))
                          for c in range(4)]
            chunks = chunk_list
            sq_done_c = {}
            sq_used_p = [0, 0]
            for ci, (srct, c, ha, hb) in enumerate(chunks):
                par = ci % 2
                sq_done_c[ci] = p.do("vector", lambda EE, s=srct, cc=c, pq=par:
                                     EE.tensor_tensor(
                    qsq[pq][:, :], s[:, cc * S:(cc + 1) * S],
                    s[:, cc * S:(cc + 1) * S], ALU.mult),
                    wait=[(vE, rope_done), (vE, v_done), (pe, sq_used_p[par])],
                    inc=vE)
                for hi, (off, col) in enumerate((ha, hb)):
                    hpar = (2 * ci + hi) % 2
                    mmb = 0
                    for tc in range(4):
                        mmb = p.do("tensor", lambda EE, o=off, t=tc,
                                   b=pb[hpar], pq=par: EE.matmul(
                                       b[0:128, t * 512:(t + 1) * 512],
                                       onesc[o:o + 64, 0:128],
                                       qsq[pq][o:o + 64, t * 512:(t + 1) * 512],
                                       start=True, stop=True),
                                   wait=([(vE, sq_done_c[ci]),
                                          (vE, pb_free[hpar])]
                                         if tc == 0 else []),
                                   inc=pe if tc == 3 else None) or mmb
                    sq_used_p[par] = mmb
                    pb_free[hpar] = last_red = p.do(
                        "vector", lambda EE, cl=col, b=pb[hpar]: EE.reduce_max(
                            n2[0:1, cl:cl + 1], b[0:1, :], axis=AX.X),
                        wait=[(pe, mmb)], inc=vE)
            p.wait("vector", vE, last_red)
        c_c2 = 0
        for kv in range(KVH):
            c_c2 = p.do("vector", lambda EE, k=kv: EE.tensor_scalar(
                n2[0:1, 4 * k:4 * k + 4], n2[0:1, 4 * k:4 * k + 4],
                n2[0:1, H + k:H + k + 1], None, ALU.mult),
                wait=[(vE, last_red)], inc=vE)
        p.wait("vector", vE, c_c2)
        c_sqt = p.do("scalar", lambda EE: EE.activation(
            n2[0:1, 0:H], n2[0:1, 0:H], AF.Sqrt), wait=[(vE, c_c2)], inc=sE)
        c_ng = p.do("vector", lambda EE: EE.tensor_scalar(
            negc[0:1, :], n2[0:1, 0:H], -0.125, None, ALU.mult),
            wait=[(sE, c_sqt)], inc=vE)
        with nc.psum_tensor(f"pcn{e}", [128, H], F32) as pcn:
            c_bc = p.do("tensor", lambda EE: EE.matmul(
                pcn[:, :], onesr[0:1, :], negc[0:1, :], start=True, stop=True),
                wait=[(vE, c_ng)], inc=pe)
            c_cneg = p.do("vector", lambda EE: EE.tensor_copy(
                cneg[:, :], pcn[:, :]), wait=[(pe, c_bc)], inc=vE)
        sc3.close()
        if dbg is not None and dbg >= 3:
            p.dma("sync", cdbg[:, :], cnegs[:, :], dH, wait=[(vE, c_cneg)])
            p.dma("sync", qdbg[e], qt[:, :], dH)
            p.dma("sync", kdbg[e], kt4[:, :], dH)
            p.dma("sync", vdbg[e], v65[:, :], dH, wait=[(vE, v_done)])
        if dbg == 3:
            scope.close()
            return snap_waits() + [(dH, dH.n), (dS, dS.n)]

        sc4 = ExitStack()
        ktz = sc4.enter_context(nc.sbuf_tensor(f"ktz{e}", [128, KVH * S], BF16))
        kz_ms = p.do("vector", lambda EE: EE.memset(ktz[:, :], 0.0),
                     wait=[(vE, v_done), (vE, c_cneg)], inc=vE)
        p.wait("vector", vE, kz_ms)
        kz_done = 0
        for kvh_ in range(KVH):
            r0 = (kvh_ % 2) * 64
            kz_done = p.do("vector", lambda EE, kv=kvh_, r=r0: EE.tensor_copy(
                ktz[r:r + 64, kv * S:(kv + 1) * S],
                kt4[r:r + 64, (kv // 2) * S:(kv // 2 + 1) * S]), inc=vE)
        otq = sc4.enter_context(nc.sbuf_tensor(f"otq{e}", [128, NHC * QC], BF16,
                                               side="right"))
        ptb = [sc4.enter_context(nc.sbuf_tensor(f"pt{e}{i}", [128, 1024], BF16,
                                                side="right")) for i in range(4)]
        rsinv = sc4.enter_context(nc.sbuf_tensor(f"rsv{e}", [1, QC], BF16,
                                                 side="right"))
        invb = sc4.enter_context(nc.sbuf_tensor(f"invb{e}", [64, QC], F32,
                                                side="right"))
        pS = [sc4.enter_context(nc.psum_tensor(f"pS{e}{i}", [128, 1024], F32))
              for i in range(2)]
        pO = [sc4.enter_context(nc.psum_tensor(f"pO{e}{i}", [128, QC], F32))
              for i in range(2)]
        pI = sc4.enter_context(nc.psum_tensor(f"pI{e}", [128, QC], F32))

        exp_done = [0, 0]
        s_done = [0, 0]
        o_mm = [0, 0]
        o_mm4 = [0, 0, 0, 0]
        mask_done = [0, 0, 0, 0]
        ot_done = [0, 0]
        inv_mm = 0
        ib_done = 0
        ri_done = 0
        otq_dma = [0]

        steps = []
        for qc in range(NQC):
            nkt = 4 * qc + 4
            for h in range(H):
                for kp in range(nkt // 2):
                    steps.append((qc, h, kp, nkt))

        def emit_S(i):
            qc, h, kp, nkt = steps[i]
            kvh = h // 4
            qch, qoff = CHUNK_OF[h], OFF_OF[h]
            spar = i % 2
            w = [(vE, kz_done), (sE, exp_done[spar]), (vE, v_done)]
            s_mm = 0
            for half in range(2):
                ki = kp * 2 + half
                s_mm = p.do("tensor", lambda EE, k=ki, hf=half, sp=spar,
                            kv=kvh, qh_=qch, qq=qc: EE.matmul(
                    pS[sp][:, hf * 512:(hf + 1) * 512],
                    ktz[:, kv * S + k * 128: kv * S + k * 128 + 128],
                    qt[:, qh_ * S + qq * QC: qh_ * S + (qq + 1) * QC],
                    start=True, stop=True), wait=w,
                    inc=pe if half == 1 else None) or s_mm
                w = []
            s_done[spar] = s_mm

        def emit_tail(i):
            qc, h, kp, nkt = steps[i]
            kvh = h // 4
            hpar = h % 2
            spar = i % 2
            ptp = i % 4
            exp_done[spar] = p.do("scalar", lambda EE, sp=spar, pt=ptp, hh=h:
                                  EE.activation(
                                      ptb[pt][:, :], pS[sp][:, :], AF.Exp,
                                      scale=0.125, bias=cneg[:, hh:hh + 1]),
                                  wait=[(pe, s_done[spar]), (pe, o_mm4[ptp])],
                                  inc=sE)
            vv = 0
            for half in range(2):
                di = kp * 2 + half - 4 * qc
                if di >= 0:
                    nc_ = 128 * (di + 1)
                    vv = p.do("vector", lambda EE, pt=ptp, hf=half, d=di,
                              n_=nc_: EE.tensor_tensor(
                                  ptb[pt][:, hf * 512: hf * 512 + n_],
                                  ptb[pt][:, hf * 512: hf * 512 + n_],
                                  stair[:, 512 - 128 * d: 512 - 128 * d + n_],
                                  ALU.mult),
                              wait=[(sE, exp_done[spar])], inc=vE)
            if vv:
                mask_done[ptp] = vv
            w = [(sE, exp_done[spar])]
            if vv:
                w.append((vE, vv))
            if kp == 0:
                w.append((vE, ot_done[hpar]))
            for half in range(2):
                ki = kp * 2 + half
                o_mm4[ptp] = p.do("tensor", lambda EE, k=ki, hp=hpar, pt=ptp,
                                  hf=half, kv=kvh, nk=nkt: EE.matmul(
                                      pO[hp][:, :],
                                      v65[:, k * 520 + kv * 65: k * 520 + kv * 65 + 128],
                                      ptb[pt][:, hf * 512:(hf + 1) * 512],
                                      start=(k == 0), stop=(k == nk - 1)),
                                  wait=w, inc=pe if half == 1 else None
                                  ) or o_mm4[ptp]
                w = []
            o_mm[spar] = o_mm4[ptp]

        def emit_recip(i):
            nonlocal ri_done
            qc, h, kp, nkt = steps[i]
            hpar = h % 2
            def _recip(EE, hp=hpar):
                with nc.allow_low_precision("bf16 inv-rowsum operand for "
                                            "K=1 broadcast matmul"):
                    return EE.reciprocal(rsinv[0:1, :], pO[hp][64:65, :])
            ri_done = p.do("vector", _recip,
                           wait=[(pe, o_mm[0]), (pe, o_mm[1]),
                                 (pe, inv_mm)], inc=vE)

        def emit_norm(i):
            nonlocal inv_mm, ib_done
            qc, h, kp, nkt = steps[i]
            qch, qoff = CHUNK_OF[h], OFF_OF[h]
            hpar = h % 2
            inv_mm = p.do("tensor", lambda EE: EE.matmul(
                pI[:, :], onesr[0:1, 0:128], rsinv[0:1, :], start=True, stop=True),
                wait=[(vE, ri_done), (vE, ib_done)], inc=pe)
            for half in range(2):
                ib_done = p.do("vector", lambda EE, hf=half: EE.tensor_copy(
                    invb[:, hf * 256:(hf + 1) * 256],
                    pI[0:64, hf * 256:(hf + 1) * 256]),
                    wait=[(pe, inv_mm)], inc=vE)
            ww = [(dH, otq_dma[0]), (vE, ib_done)]
            for half in range(2):
                ot_done[hpar] = p.do(
                    "vector", lambda EE, hp=hpar, qh=qch, qo=qoff, hf=half:
                    EE.tensor_tensor(
                        otq[qo:qo + 64, qh * QC + hf * 256: qh * QC + (hf + 1) * 256],
                        pO[hp][0:64, hf * 256:(hf + 1) * 256],
                        invb[:, hf * 256:(hf + 1) * 256], ALU.mult),
                    wait=ww, inc=vE)
                ww = []

        nsteps = len(steps)
        for i in range(nsteps + 2):
            if i < nsteps:
                emit_S(i)
            if i >= 1 and i - 1 < nsteps:
                emit_tail(i - 1)
                qc, h, kp, nkt = steps[i - 1]
                if kp == nkt // 2 - 1:
                    emit_recip(i - 1)
            if i >= 2 and i - 2 < nsteps:
                j = i - 2
                qc, h, kp, nkt = steps[j]
                if kp == nkt // 2 - 1:
                    emit_norm(j)
                    if h == H - 1:
                        ot_all = max(ot_done)
                        otq_dma[0] = p.dma("sync", ot_scr[e, qc], otq[:, :],
                                           dH, wait=[(vE, ot_all)])
        sc4.close()
        scope.close()
        return snap_waits() + [(dH, dH.n), (dS, dS.n)]

    snap_prev = []
    for e in range(E):
        snap_prev = p1_p2_elem(e)
    barrier()
    if dbg is not None and dbg < 6:
        p.wait("sync", dH, dH.n)
        p.wait("gpsimd", dS, dS.n)
        p.wait("scalar", dA, dA.n)
        for s_ in all_sems:
            p.do("sync", lambda EE, hh=s_.h: EE.sem_clear(hh))
        p.emit(nc)
        es.close()
        return nc

    p12c.close()
    NR = 12
    PRE = 8
    CB = 4
    sc5 = ExitStack()
    fuse = sc5.enter_context(nc.sbuf_tensor("fuse", [128, NJC * 512], BF16))
    sdb_sb = sc5.enter_context(nc.sbuf_tensor("sdb_sb", [128, D], F32))
    p.dma("sync", sdb_sb[:, :], sdb_in[:, :], dH)
    wgu = sc5.enter_context(nc.sbuf_tensor("wgu", [128, NJC * 512], BF16))
    xn_tb = [sc5.enter_context(nc.sbuf_tensor("xntb0", [128, NHC * 512], BF16))]
    xn_tb.append(xn_tb[0])
    otb = sc5.enter_context(nc.sbuf_tensor("otb", [128, NHC * 512], BF16))
    ring = sc5.enter_context(nc.sbuf_tensor("ring", [128, NR * 512], BF16))
    usb = [sc5.enter_context(nc.sbuf_tensor(f"usb{i}", [128, 512], F32))
           for i in range(2)]
    silb = [sc5.enter_context(nc.sbuf_tensor(f"silb{i}", [128, 512], F32))
            for i in range(2)]
    xtile = [sc5.enter_context(nc.sbuf_tensor(f"xt3{i}", [128, 512], F32))
             for i in range(4)]
    dtile = [sc5.enter_context(nc.sbuf_tensor(f"dt3{i}", [128, 512], F32))
             for i in range(4)]
    pg = [sc5.enter_context(nc.psum_tensor(f"pg{i}", [128, 512], F32))
          for i in range(2)]
    pu = [sc5.enter_context(nc.psum_tensor(f"pu{i}", [128, 512], F32))
          for i in range(2)]
    pj4 = [sc5.enter_context(nc.psum_tensor(f"pj{i}", [128, 512], F32))
           for i in range(4)]
    pd4 = [pg[0], pg[1], pu[0], pu[1]]

    def wg_view(par):
        return wgu[:, par * 16 * 512:(par + 1) * 16 * 512]

    def wu_view(par):
        return wgu[:, (2 + par) * 16 * 512:(3 + par) * 16 * 512]

    xn_dma = [0, 0]
    span_dma = [0, 0]
    span_free = [0, 0]
    sil_done = [0, 0]
    u_done = [0, 0]
    fuse_written = [0] * NJC
    gu_last_mm = 0
    fuse_free = 0
    pj_last = 0
    pd_free = 0
    pj_free = 0
    ca_tt = [0, 0, 0, 0]
    cb_tt = [0, 0, 0, 0]
    gadd_tt = [0, 0, 0, 0]
    o_dma_tt = [0, 0, 0, 0]
    x_dma_tt = [0, 0, 0, 0]
    dstop = [0, 0, 0, 0]
    jstop = [0, 0, 0, 0]
    slot_user = [0] * NR
    g_k = [0]
    inflight = deque()

    def emit_load(spec, b_):
        a_, kind, c_ = spec
        slot = g_k[0] % NR
        g_k[0] += 1
        src = (wd_bf[a_][:, c_ * 512:(c_ + 1) * 512] if kind == 0
               else wot_bf[a_][:, c_ * 512:(c_ + 1) * 512])
        w = [(pe, slot_user[slot])]
        if b_ == 0:
            w.append((dS, conv_state[0]))
        dv = p.dma("sync", ring[:, slot * 512:(slot + 1) * 512], src,
                   rS[slot], wait=w)
        inflight.append((slot, dv))

    for b in range(NBLK):
        e = b // (NBLK // E)
        tb = b % (NBLK // E)
        bpar = b % 2
        w = [(pe, gu_last_mm)] if b >= 1 else []
        for hh in range(2):
            xn_dma[bpar] = p.dma(
                "sync",
                xn_tb[bpar][:, hh * 8 * 512:(hh + 1) * 8 * 512].rearrange(
                    "p (c t) -> p c t", t=512),
                xnt_scr[e].rearrange("c p t -> p c t")[
                    :, hh * 8:(hh + 1) * 8, tb * 512:(tb + 1) * 512],
                dH, wait=w)
            w = []
        otb_dma = p.dma("sync", otb[:, :], ot_scr[e, tb], dH,
                        wait=[(pe, pj_last)] if b >= 1 else [])
        loads = deque()
        for a_ in range(4):
            for c_ in range(NJC):
                loads.append((a_, 0, c_))
            for c_ in range(NHC):
                loads.append((a_, 1, c_))
        for _ in range(PRE):
            emit_load(loads.popleft(), b)
        for s_ in range(16):
            spar = s_ % 2
            w = [(pe, span_free[spar]), (dS, conv_state[0])]
            for (wdram, view) in ((wg_bf, wg_view(spar)), (wu_bf, wu_view(spar))):
                span_dma[spar] = p.dma("sync", view[:, :], wdram[s_], dH, wait=w)
                w = []
            for j4 in range(4):
                jc = s_ * 4 + j4
                gpar = jc % 2
                wmm = [(dH, span_dma[spar]), (dH, xn_dma[bpar]),
                       (sE, sil_done[gpar])]
                if s_ == 0:
                    wmm.append((vE, ca_tt[gpar]))
                gm = 0
                for hc in range(NHC):
                    gm = p.do("tensor", lambda EE, h=hc, j=j4, sp=spar, gp=gpar,
                              bp=bpar:
                              EE.matmul(
                                  pg[gp][:, :],
                                  wg_view(sp)[:, h * 512 + j * 128: h * 512 + j * 128 + 128],
                                  xn_tb[bp][:, h * 512:(h + 1) * 512],
                                  start=(h == 0), stop=(h == NHC - 1)),
                              wait=wmm, inc=pe if hc == NHC - 1 else None) or gm
                    wmm = []
                wmm = [(sE, u_done[gpar])]
                if s_ == 0:
                    wmm.append((vE, ca_tt[2 + gpar]))
                um = 0
                for hc in range(NHC):
                    um = p.do("tensor", lambda EE, h=hc, j=j4, sp=spar, gp=gpar,
                              bp=bpar:
                              EE.matmul(
                                  pu[gp][:, :],
                                  wu_view(sp)[:, h * 512 + j * 128: h * 512 + j * 128 + 128],
                                  xn_tb[bp][:, h * 512:(h + 1) * 512],
                                  start=(h == 0), stop=(h == NHC - 1)),
                              wait=wmm, inc=pe if hc == NHC - 1 else None) or um
                    wmm = []
                gu_last_mm = um
                if j4 == 3:
                    span_free[spar] = um
                sil_done[gpar] = p.do("scalar", lambda EE, gp=gpar, j=jc:
                                      EE.activation(
                                          silb[gp][:, :], pg[gp][:, :], AF.Silu,
                                          scale=sg_sb[:, j:j + 1]),
                                      wait=[(pe, gm), (vE, fuse_written[jc - 2] if jc >= 2 else 0)],
                                      inc=sE)
                u_done[gpar] = p.do("scalar", lambda EE, gp=gpar, j=jc:
                                    EE.activation(
                                        usb[gp][:, :], pu[gp][:, :], AF.Identity,
                                        scale=su_sb[:, j:j + 1]),
                                    wait=[(pe, um), (vE, fuse_written[jc - 2] if jc >= 2 else 0)],
                                    inc=sE)
                fuse_written[jc] = p.do("vector", lambda EE, gp=gpar, j=jc:
                                        EE.tensor_tensor(
                                            fuse[:, j * 512:(j + 1) * 512],
                                            usb[gp][:, :], silb[gp][:, :], ALU.mult),
                                        wait=[(sE, sil_done[gpar]), (sE, u_done[gpar]),
                                              (pe, fuse_free if b > 0 else 0)],
                                        inc=vE)
        fuse_all = fuse_written[NJC - 1]
        for a in range(4):
            for c0 in range(0, NJC, CB):
                batch = [inflight.popleft() for _ in range(CB)]
                fin = (c0 == NJC - CB)
                lastu = [0] * CB
                for tt in range(4):
                    for cb in range(CB):
                        slot, dv = batch[cb]
                        w = []
                        if tt == 0:
                            w.append((rS[slot], dv))
                            if c0 == 0 and cb == 0:
                                w += [(vE, fuse_all), (vE, pd_free)]
                        cc = c0 + cb
                        last = p.do("tensor", lambda EE, sl=slot, t=tt, c_=cc:
                                    EE.matmul(
                                        pd4[t][:, :],
                                        fuse[:, c_ * 512 + t * 128: c_ * 512 + t * 128 + 128],
                                        ring[:, sl * 512:(sl + 1) * 512],
                                        start=(c_ == 0), stop=(c_ == NJC - 1)),
                                    wait=w,
                                    inc=pe if (tt == 3 or (fin and cb == CB - 1))
                                    else None)
                        if tt == 3:
                            lastu[cb] = last
                        if fin and cb == CB - 1:
                            dstop[tt] = last
                for cb in range(CB):
                    slot_user[batch[cb][0]] = lastu[cb]
                for _ in range(CB):
                    if loads:
                        emit_load(loads.popleft(), b)
            for tt in range(4):
                ca_tt[tt] = p.do("vector", lambda EE, t=tt, aa=a:
                                 EE.tensor_tensor(
                                     dtile[t][:, :], pd4[t][:, :],
                                     sdb_sb[:, aa * 512:(aa + 1) * 512],
                                     ALU.mult),
                                 wait=[(pe, dstop[tt]), (dH, o_dma_tt[tt])],
                                 inc=vE)
            pd_free = ca_tt[3]
            for o0 in range(0, NHC, CB):
                tx = o0 // CB
                x_dma_tt[tx] = p.dma(
                    "sync", xtile[tx][:, :],
                    x_in[e, tb * 512 + tx * 128: tb * 512 + tx * 128 + 128,
                         a * 512:(a + 1) * 512],
                    dH, wait=[(gE, gadd_tt[tx])])
                batch = [inflight.popleft() for _ in range(CB)]
                fin = (o0 == NHC - CB)
                lastu = [0] * CB
                for tt in range(4):
                    for cb in range(CB):
                        slot, dv = batch[cb]
                        w = []
                        if tt == 0:
                            w.append((rS[slot], dv))
                            if o0 == 0 and cb == 0:
                                w += [(vE, pj_free), (dH, otb_dma)]
                        oo = o0 + cb
                        last = p.do("tensor", lambda EE, sl=slot, t=tt, o_=oo:
                                    EE.matmul(
                                        pj4[t][:, :],
                                        otb[:, o_ * 512 + t * 128: o_ * 512 + t * 128 + 128],
                                        ring[:, sl * 512:(sl + 1) * 512],
                                        start=(o_ == 0), stop=(o_ == NHC - 1)),
                                    wait=w,
                                    inc=pe if (tt == 3 or (fin and cb == CB - 1))
                                    else None)
                        if tt == 3:
                            lastu[cb] = last
                        if fin and cb == CB - 1:
                            jstop[tt] = last
                for cb in range(CB):
                    slot_user[batch[cb][0]] = lastu[cb]
                for _ in range(CB):
                    if loads:
                        emit_load(loads.popleft(), b)
            pj_last = jstop[3]
            for tt in range(4):
                cb_tt[tt] = p.do("vector", lambda EE, t=tt: EE.tensor_tensor(
                    xtile[t][:, :], xtile[t][:, :], pj4[t][:, :], ALU.add),
                    wait=[(dH, x_dma_tt[tt]), (pe, jstop[tt])], inc=vE)
                gadd_tt[tt] = p.do("gpsimd", lambda EE, t=tt: EE.tensor_tensor(
                    dtile[t][:, :], dtile[t][:, :], xtile[t][:, :], ALU.add),
                    wait=[(vE, ca_tt[tt]), (vE, cb_tt[tt])], inc=gE)
                o_dma_tt[tt] = p.dma(
                    "sync",
                    out[e, tb * 512 + tt * 128: tb * 512 + tt * 128 + 128,
                        a * 512:(a + 1) * 512],
                    dtile[tt][:, :], dH, wait=[(gE, gadd_tt[tt])])
            pj_free = cb_tt[3]
        fuse_free = dstop[3]
    sc5.close()

    p.wait("sync", dH, dH.n)
    p.wait("gpsimd", dS, dS.n)
    p.wait("scalar", dA, dA.n)
    barrier()
    for s_ in all_sems:
        p.do("sync", lambda EE, hh=s_.h: EE.sem_clear(hh))

    p.emit(nc)
    es.close()
    return nc



def _host_constants():
    ident = np.eye(128, dtype=BF)
    r2 = np.zeros((128, 128), dtype=np.float32)
    for base in (0, 64):
        for i in range(32):
            r2[base + i, base + i + 32] = -1.0
            r2[base + 32 + i, base + i] = 1.0
    r2t = np.ascontiguousarray(r2.T).astype(BF)
    inv_freq = (1.0 / (10000.0 ** (np.arange(0, HD, 2, dtype=np.float32) / HD))
                ).astype(np.float32)
    pos = np.arange(S, dtype=np.float32)
    freqs = pos[:, None] * inv_freq[None, :]
    emb = np.concatenate([freqs, freqs], axis=1)
    cos = np.cos(emb).astype(np.float32).T
    sin = np.sin(emb).astype(np.float32).T
    cosT = np.ascontiguousarray(np.concatenate([cos, cos], axis=0))
    sinT = np.ascontiguousarray(np.concatenate([sin, sin], axis=0))
    kk = np.arange(128)[:, None]
    cc = np.arange(1024)[None, :]
    stair = ((cc - 512) >= kk).astype(BF)
    onesr = np.ones((1, 128), dtype=BF)
    onesc = np.zeros((128, 128), dtype=BF)
    onesc[:, 0] = 1
    return dict(ident=ident, r2t=r2t, cost=cosT, sint=sinT, stair=stair,
                onesr=onesr, onesc=onesc)


def kernel(L0_attn_hidden_states, L0_attn_Wq, L0_attn_Wk, L0_attn_Wv,
           L0_attn_Wo, L0_ffn_w_gate, L0_ffn_w_up, L0_ffn_w_down,
           L0_ffn_s_gate, L0_ffn_s_up, L0_ffn_s_down):
    x = np.asarray(L0_attn_hidden_states)
    wq = np.asarray(L0_attn_Wq)
    wk = np.asarray(L0_attn_Wk)
    wv = np.asarray(L0_attn_Wv)
    wo = np.asarray(L0_attn_Wo)
    wg = np.ascontiguousarray(np.asarray(L0_ffn_w_gate))
    wu = np.ascontiguousarray(np.asarray(L0_ffn_w_up))
    wd = np.ascontiguousarray(np.asarray(L0_ffn_w_down))
    sg = np.asarray(L0_ffn_s_gate)
    su = np.asarray(L0_ffn_s_up)
    sd = np.asarray(L0_ffn_s_down)

    perm = _qhead_perm()
    wqt = np.ascontiguousarray(wq.T[:, perm])
    wkt = np.ascontiguousarray(wk.T)
    wvt = np.ascontiguousarray(wv.T)
    wot = np.ascontiguousarray(wo.T[perm, :])
    sg_h = np.ascontiguousarray(sg.reshape(NJC, 128).T)
    su_h = np.ascontiguousarray(su.reshape(NJC, 128).T)
    sdb = np.ascontiguousarray(np.broadcast_to(sd[None, :], (128, D)))

    consts = _host_constants()
    nc = build_nc()

    base = dict(wqt=wqt, wkt=wkt, wvt=wvt, wot=wot, wg=wg, wu=wu, wd=wd,
                sg=sg_h, su=su_h, sdb=sdb, **consts)
    in_maps = []
    for core in range(NCORES):
        m = dict(base)
        m["x"] = x[core * E:(core + 1) * E]
        in_maps.append(m)

    res = run_bass_kernel_spmd(nc, in_maps, core_ids=list(range(NCORES)))
    outs = [np.asarray(r["out"]) for r in res.results]
    return np.concatenate(outs, axis=0)
